# version 04
# baseline (speedup 1.0000x reference)
import numpy as np
import ml_dtypes
import concourse.bass as bass
import concourse.mybir as mybir
from concourse.alu_op_type import AluOpType as OP
from concourse.bass_utils import run_bass_kernel_spmd
from concourse.aot_env import allow_ici_all_to_all
from contextlib import ExitStack

F32 = mybir.dt.float32
BF16 = mybir.dt.bfloat16
I32 = mybir.dt.int32
AF = mybir.ActivationFunctionType
AX = mybir.AxisListType

D = 1024
T = 2048
S = 4096
NB = 16
H = 16
HD = 64
DI = 2048
DIH = 1024
NSTATE = 16
DTR = 64
E = 16
DFF = 512

STOP_AFTER = 99
ATT_SUB = 99
ATT_HEADS = 16
SKIP_P1 = False
DBG = None

ENGS = ("tensor", "vector", "scalar", "gpsimd", "sync")
DMAQ = {"sync": "dq_sync", "scalar": "dq_scal", "gpsimd": "dq_gp"}


class Prog:

    def __init__(self):
        self.rec = []
        self.keys = {}
        self.eng_pos = {e: [] for e in ENGS}

    def emit(self, eng, fn, reads=(), writes=(), kind="op", tag=""):
        idx = len(self.rec)
        deps = set()
        for k in reads:
            st = self.keys.get(k)
            if st and st["w"] is not None:
                deps.add(st["w"])
        for k in writes:
            st = self.keys.get(k)
            if st:
                if st["w"] is not None:
                    deps.add(st["w"])
                deps.update(st["r"])
        deps.discard(idx)
        r = dict(eng=eng, fn=fn, kind=kind, deps=deps, idx=idx,
                 flagged=(kind != "op"), tag=tag, reads=tuple(reads),
                 writes=tuple(writes))
        self.rec.append(r)
        self.eng_pos[eng].append(idx)
        for k in reads:
            self.keys.setdefault(k, {"w": None, "r": []})["r"].append(idx)
        for k in writes:
            self.keys[k] = {"w": idx, "r": []}
        return idx

    def prepare(self):
        for r in self.rec:
            for d in r["deps"]:
                dr = self.rec[d]
                if dr["kind"] == "op" and (dr["eng"] != r["eng"]
                                           or r["eng"] != "tensor"):
                    dr["flagged"] = True
        self.ccval = {}
        cccnt = 0
        for r in self.rec:
            if r["kind"] == "cc":
                cccnt += 1
                self.ccval[r["idx"]] = cccnt
        self.opval = {}
        self.op_prewait = {}
        for q in ENGS:
            ops = [r["idx"] for r in self.rec
                   if r["kind"] == "op" and r["eng"] == q and r["flagged"]]
            if not ops:
                continue
            oset = set(ops)
            groups = []
            cur = []
            curset = set()
            need_close = False
            for r in self.rec:
                if r["idx"] in oset:
                    if (need_close or any(d in curset for d in r["deps"])) \
                            and cur:
                        groups.append(cur)
                        cur = []
                        curset = set()
                        need_close = False
                    cur.append(r["idx"])
                    curset.add(r["idx"])
                else:
                    if any(d in curset for d in r["deps"]):
                        need_close = True
            if cur:
                groups.append(cur)
            tot = 0
            for grp in groups:
                if tot > 0:
                    self.op_prewait[grp[0]] = ("pg_" + q, tot)
                tot += len(grp)
                for d in grp:
                    self.opval[d] = tot
        NSEM = 4
        self.dma_sem = {}
        self.dma_wval = {}
        self.dma_prewait = {}
        for q in ("sync", "scalar", "gpsimd"):
            dmas = [r["idx"] for r in self.rec
                    if r["kind"] == "dma" and r["eng"] == q]
            if not dmas:
                continue
            dset = set(dmas)
            closers = set()
            for r in self.rec:
                if r["kind"] == "dma":
                    continue
                for d in r["deps"]:
                    if d in dset:
                        closers.add(d)
            groups = []
            cur = []
            closed_at = -1
            for r in self.rec:
                if r["idx"] in dset:
                    cur.append(r["idx"])
                elif cur and any(d in closers and d in cur for d in r.get("deps", ())):
                    groups.append(cur)
                    cur = []
            groups = []
            cur = []
            need_close = False
            for r in self.rec:
                if r["idx"] in dset:
                    if (need_close or any(d in cur for d in r["deps"])) and cur:
                        groups.append(cur)
                        cur = []
                        need_close = False
                    cur.append(r["idx"])
                else:
                    if any(d in cur for d in r["deps"]):
                        need_close = True
            if cur:
                groups.append(cur)
            semtot = {}
            for gi, grp in enumerate(groups):
                snm = "dq_%s_%d" % (q, gi % NSEM)
                tot = semtot.get(snm, 0) + 16 * len(grp)
                prev = semtot.get(snm, 0)
                if prev > 0:
                    self.dma_prewait[grp[0]] = (snm, prev)
                semtot[snm] = tot
                for d in grp:
                    self.dma_sem[d] = snm
                    self.dma_wval[d] = tot

    def emit_stream(self, e, eng, sems):
        wm = {}

        def sem_of(dep):
            dr = self.rec[dep]
            if dr["kind"] == "dma":
                return self.dma_sem[dep], self.dma_wval[dep]
            if dr["kind"] == "cc":
                return "cc", self.ccval[dep]
            return "pg_" + dr["eng"], self.opval[dep]

        for ri in self.eng_pos[e]:
            r = self.rec[ri]
            waits = {}
            for d in sorted(r["deps"]):
                dr = self.rec[d]
                if dr["eng"] == e and dr["kind"] == "op" and e == "tensor":
                    continue
                s_, v = sem_of(d)
                if wm.get(s_, 0) < v:
                    waits[s_] = max(waits.get(s_, 0), v)
            if r["idx"] in self.op_prewait:
                s2, v2 = self.op_prewait[r["idx"]]
                if wm.get(s2, 0) < v2:
                    waits[s2] = max(waits.get(s2, 0), v2)
            for s_, v in waits.items():
                eng.wait_ge(sems[s_], v)
                wm[s_] = v
            if r["kind"] == "dma" and r["idx"] in self.dma_prewait:
                s2, v2 = self.dma_prewait[r["idx"]]
                if wm.get(s2, 0) < v2:
                    eng.wait_ge(sems[s2], v2)
                    wm[s2] = v2
            inst = r["fn"](eng)
            if r["kind"] == "dma":
                inst.then_inc(sems[self.dma_sem[r["idx"]]], 16)
            elif r["kind"] == "cc":
                inst.then_inc(sems["cc"], 1)
            elif r["flagged"]:
                inst.then_inc(sems["pg_" + e], 1)

    def barrier(self):
        ph = len(self.rec)
        for e in ENGS:
            self.emit(e, lambda eng: eng.nop(), reads=(),
                      writes=("_bar_%d_%s" % (ph, e),), tag="bar")
        for e in ENGS:
            self.emit(e, lambda eng: eng.nop(),
                      reads=tuple("_bar_%d_%s" % (ph, x) for x in ENGS),
                      writes=(), tag="bar2")


def _shard_host(inputs):
    x = np.ascontiguousarray(inputs["L0_attn_x"])
    per_core = []
    cc = np.ascontiguousarray

    shared = {}
    shared["wq_t"] = cc(inputs["L0_attn_Wq"].T)
    shared["wk_t"] = cc(inputs["L0_attn_Wk"].T)
    shared["wv_t"] = cc(inputs["L0_attn_Wv"].T)
    shared["wo_t"] = cc(inputs["L0_attn_Wo"].T)
    shared["l0w"] = cc(inputs["L0_ffn_weight"])
    shared["l0b"] = cc(inputs["L0_ffn_bias"])
    shared["gate_wt"] = cc(inputs["L1_ffn_gate_w"].T)
    shared["ebias"] = cc(inputs["L1_ffn_expert_bias"])
    w1t = inputs["L1_ffn_w1"].transpose(0, 2, 1)
    w3t = inputs["L1_ffn_w3"].transpose(0, 2, 1)
    w2t = inputs["L1_ffn_w2"].transpose(0, 2, 1)
    shared["w1t"] = cc(w1t)
    shared["w3t"] = cc(w3t)
    shared["w2t"] = cc(w2t)
    shared["b1"] = cc(inputs["L1_ffn_b1"])
    shared["b3"] = cc(inputs["L1_ffn_b3"])
    shared["b2"] = cc(inputs["L1_ffn_b2"])
    shared["sw1t"] = cc(inputs["L1_ffn_sw1"].T)
    shared["sw3t"] = cc(inputs["L1_ffn_sw3"].T)
    shared["sw2t"] = cc(inputs["L1_ffn_sw2"].T)
    shared["sb1"] = cc(inputs["L1_ffn_sb1"])
    shared["sb3"] = cc(inputs["L1_ffn_sb3"])
    shared["sb2"] = cc(inputs["L1_ffn_sb2"])
    shared["win"] = cc(inputs["L1_attn_W_in"])
    shared["convw"] = cc(inputs["L1_attn_conv_w"])
    shared["convb"] = cc(inputs["L1_attn_conv_b"])
    shared["wxp"] = cc(inputs["L1_attn_W_xproj"])
    shared["wdt"] = cc(inputs["L1_attn_W_dt"])
    shared["bdt"] = cc(inputs["L1_attn_b_dt"])
    shared["b1c"] = cc(inputs["L1_ffn_b1"].reshape(E, 4, 128)
                       .transpose(2, 0, 1).reshape(128, 64))
    shared["b3c"] = cc(inputs["L1_ffn_b3"].reshape(E, 4, 128)
                       .transpose(2, 0, 1).reshape(128, 64))
    shared["cwc"] = cc(inputs["L1_attn_conv_w"].reshape(16, 128, 4)
                       .transpose(1, 0, 2).reshape(128, 64))
    shared["cbc"] = cc(inputs["L1_attn_conv_b"].reshape(16, 128).T)
    shared["bdtc"] = cc(inputs["L1_attn_b_dt"].reshape(16, 128).T)
    shared["sb1c"] = cc(inputs["L1_ffn_sb1"].reshape(8, 128).T)
    shared["sb3c"] = cc(inputs["L1_ffn_sb3"].reshape(8, 128).T)

    alog = inputs["L1_attn_A_log"]
    dp = inputs["L1_attn_Dp"]
    wout = inputs["L1_attn_W_out"]

    for c in range(8):
        seq, g = c >> 1, c & 1
        xs = x[seq].reshape(32, 128, D)[g::2]
        x_fm = cc(xs.reshape(T, D).T)
        d = dict(shared)
        d["x_fm"] = x_fm
        d["gvec"] = np.array([g, 1.0 - g, g * 128.0, 0.0], dtype=np.float32)
        lo, hi = g * DIH, (g + 1) * DIH
        d["alog_s"] = cc(alog[lo:hi])
        d["alc"] = cc(alog[lo:hi].reshape(8, 128, NSTATE)
                      .transpose(1, 0, 2).reshape(128, 8 * NSTATE))
        d["dpc"] = cc(dp[lo:hi].reshape(8, 128).T)
        d["dp_s"] = cc(dp[lo:hi])
        d["wout_s"] = cc(wout[lo:hi])
        per_core.append(d)
    return per_core


def _gather_host(results):
    out = np.empty((4, S, D), dtype=np.float32)
    for c in range(8):
        seq, g = c >> 1, c & 1
        o = np.asarray(results[c]["out_fm"])
        ot = o.T.reshape(NB, 128, D)
        out[seq].reshape(32, 128, D)[g::2] = ot
    return out


def build_nc():
    nc = bass.Bass("TRN2", target_bir_lowering=False, debug=False)
    p = Prog()

    dI = {}
    def din(name, shape):
        dI[name] = nc.dram_tensor(name, list(shape), F32, kind="ExternalInput")
        return dI[name]
    for nm, shp in [
        ("x_fm", (D, T)), ("gvec", (4,)),
        ("wq_t", (D, D)), ("wk_t", (D, D)), ("wv_t", (D, D)), ("wo_t", (D, D)),
        ("l0w", (D, D)), ("l0b", (D,)),
        ("gate_wt", (D, E)), ("ebias", (E,)),
        ("w1t", (E, D, DFF)), ("w3t", (E, D, DFF)), ("w2t", (E, DFF, D)),
        ("b1", (E, DFF)), ("b3", (E, DFF)), ("b2", (E, D)),
        ("sw1t", (D, D)), ("sw3t", (D, D)), ("sw2t", (D, D)),
        ("sb1", (D,)), ("sb3", (D,)), ("sb2", (D,)),
        ("win", (D, 2 * DI)), ("convw", (DI, 4)), ("convb", (DI,)),
        ("wxp", (DI, 96)), ("wdt", (DTR, DI)), ("bdt", (DI,)),
        ("alog_s", (DIH, NSTATE)), ("dp_s", (DIH,)), ("wout_s", (DIH, D)),
        ("b1c", (128, 64)), ("b3c", (128, 64)), ("cwc", (128, 64)),
        ("cbc", (128, 16)), ("bdtc", (128, 16)), ("sb1c", (128, 8)),
        ("sb3c", (128, 8)), ("alc", (128, 8 * NSTATE)), ("dpc", (128, 8)),
    ]:
        din(nm, shp)
    out_fm = nc.dram_tensor("out_fm", [D, T], F32, kind="ExternalOutput")

    kv_agin = nc.dram_tensor("kv_agin", [2, 128, 16384], BF16)
    kv_agout = nc.dram_tensor("kv_agout", [2, 2, 128, 16384], BF16)
    st_xin = nc.dram_tensor("st_xin", [DI, T], BF16)
    hal_in = nc.dram_tensor("hal_in", [DI, 48], BF16)
    hal_out = nc.dram_tensor("hal_out", [2, DI, 48], BF16)
    bc_in = nc.dram_tensor("bc_in", [32, T], BF16)
    bc_out = nc.dram_tensor("bc_out", [2, 32, T], BF16)
    rope_dr = nc.dram_tensor("rope_dr", [2, 2, 32, 2, 2048], F32)
    ln_dr = nc.dram_tensor("ln_dr", [2, 2048], F32)
    gw_dr = nc.dram_tensor("gw_dr", [16, 2048], BF16)
    bc_rep = nc.dram_tensor("bc_rep", [32, 4096], BF16)
    d_dr = nc.dram_tensor("d_dr", [16, 2048], F32)
    mz_in = [nc.dram_tensor("mz_in%d" % i, [2, 3, DIH, 512], BF16) for i in range(4)]
    mz_out = [nc.dram_tensor("mz_out%d" % i, [2, 2, 3, DIH, 512], BF16) for i in range(4)]
    wo_in = [nc.dram_tensor("wo_in%d" % i, [2, D, 512], BF16) for i in range(4)]
    wo_out = [nc.dram_tensor("wo_out%d" % i, [2, 2, D, 512], BF16) for i in range(4)]

    es = ExitStack()
    sb = lambda n, shp, dt: es.enter_context(nc.sbuf_tensor(n, list(shp), dt))
    x = sb("x", (128, 16384), F32)
    slotA = sb("slotA", (128, 16384), BF16)
    slotB = sb("slotB", (128, 16384), BF16)
    wA = sb("wA", (128, 8192), BF16)
    wdT = sb("wdT", (128, 8192), BF16)
    scrF = sb("scrF", (128, 4096), F32)
    scrG = sb("scrG", (128, 2048), F32)
    rowsall = sb("rowsall", (128, 2048), F32)
    brow = sb("brow", (128, 2048), BF16)
    ones_bf = sb("ones_bf", (1, 512), BF16)
    onesc_bf = sb("onesc_bf", (128, 1), BF16)
    ident = sb("ident", (128, 128), BF16)
    gcols = sb("gcols", (128, 4), F32)
    masks = sb("masks", (128, 256), BF16)
    small = sb("small", (128, 1024), F32)
    hcarry = sb("hcarry", (128, 256), F32)
    pm = [es.enter_context(nc.psum_tensor("pm%d" % i, [128, 512], F32))
          for i in range(8)]

    sems = {}
    for nm in (["pg_tensor", "pg_vector", "pg_scalar", "pg_gpsimd", "pg_sync",
                "cc"]
               + ["dq_%s_%d" % (q, i) for q in ("sync", "scalar", "gpsimd")
                  for i in range(4)]):
        sems[nm] = es.enter_context(nc.semaphore(nm))

    def T_(fn, r=(), w=(), tag=""): p.emit("tensor", fn, r, w, "op", tag)
    def V_(fn, r=(), w=(), tag=""): p.emit("vector", fn, r, w, "op", tag)
    def SC(fn, r=(), w=(), tag=""): p.emit("scalar", fn, r, w, "op", tag)
    def G_(fn, r=(), w=(), tag=""): p.emit("gpsimd", fn, r, w, "op", tag)
    def DMA(eng, out_ap, in_ap, r=(), w=(), tag="", slow=False):
        p.emit(eng, lambda e, o=out_ap, i=in_ap, s_=slow: e.dma_start(
            out=o, in_=i, **({"allow_slow_non_contiguous": True} if s_ else {})),
               r, w, "dma", tag)
    def CC(kind, op, ins, outs, r=(), w=(), tag="cc"):
        def f(e, k=kind, o=op, i=ins, u=outs):
            with allow_ici_all_to_all():
                return e.collective_compute(
                    k, o, replica_groups=[[0, 1], [2, 3], [4, 5], [6, 7]],
                    ins=[a.opt() for a in i], outs=[a.opt() for a in u])
        p.emit("gpsimd", f, r, w, "cc", tag)

    def vcopy(out, in_, r, w, eng="vector"):
        p.emit(eng, lambda e, o=out, i=in_: e.tensor_copy(o, i), r, w, "op")
    def vts(out, in0, s1, op0, s2=None, op1=None, r=(), w=(), eng="vector"):
        def f(e, o=out, i=in0, a=s1, b=s2, o0=op0, o1=op1):
            return e.tensor_scalar(out=o, in0=i, scalar1=a, scalar2=b,
                                   op0=o0, **({"op1": o1} if o1 is not None else {}))
        p.emit(eng, f, r, w, "op")
    def vtt(out, in0, in1, op, r=(), w=(), eng="vector"):
        p.emit(eng, lambda e, o=out, a=in0, b=in1, q=op: e.tensor_tensor(
            out=o, in0=a, in1=b, op=q), r, w, "op")
    def vstt(out, in0, s, in1, op0, op1, r=(), w=(), eng="vector"):
        p.emit(eng, lambda e, o=out, a=in0, c=s, b=in1, q0=op0, q1=op1:
               e.scalar_tensor_tensor(out=o, in0=a, scalar=c, in1=b,
                                      op0=q0, op1=q1), r, w, "op")
    def act(out, in_, func, bias=None, scale=None, r=(), w=()):
        def f(e, o=out, i=in_, fu=func, b=bias, s=scale):
            kw = {}
            if b is not None:
                kw["bias"] = b
            if s is not None:
                kw["scale"] = s
            return e.activation(out=o, in_=i, func=fu, **kw)
        p.emit("scalar", f, r, w, "op")
    def mm(out, lhsT, rhs, start, stop, r=(), w=()):
        p.emit("tensor", lambda e, o=out, l=lhsT, m=rhs, s1=start, s2=stop:
               e.matmul(o, l, m, start=s1, stop=s2), r, w, "op")
    def tpose(out, in_, idn, r=(), w=()):
        p.emit("tensor", lambda e, o=out, i=in_, d=idn: e.transpose(o, i, d),
               r, w, "op")
    def red(out, in_, axis, op, absval=None, r=(), w=(), eng="vector"):
        def f(e, o=out, i=in_, a=axis, q=op, v=absval):
            kw = {"apply_absolute_value": v} if v is not None else {}
            return e.tensor_reduce(out=o, in_=i, axis=a, op=q, **kw)
        p.emit(eng, f, r, w, "op")

    identf = sb("identf", (128, 128), F32)
    qtm = sb("qtm", (128, 1024), BF16)
    scrG_i = None
    _sc = [528]
    def smalloc(n):
        a = _sc[0]; _sc[0] += n
        assert _sc[0] <= 1024
        return a
    C_QMX = smalloc(32); C_QS = smalloc(32)
    C_MS = smalloc(16); C_SQ = smalloc(16)
    C_MC = smalloc(16); C_SC = smalloc(16)
    C_T1 = smalloc(16); C_T2 = smalloc(16); C_T3 = smalloc(16)
    C_VAR = smalloc(16); C_INV = smalloc(16)
    C_SGN = smalloc(1)
    C_R75 = smalloc(1); C_QIV = smalloc(32)

    slotA_f = slotA.ap().bitcast(F32)
    slotB_f = slotB.ap().bitcast(F32)
    scrF_bf = scrF.ap().bitcast(BF16)
    scrG_bf = scrG.ap().bitcast(BF16)
    scrG_i = scrG.ap().bitcast(I32)
    slotB_b = slotB.ap()

    SWAPM = [i + 1 if i % 2 == 0 else i - 1 for i in range(32)]
    TP2 = 6.28125
    TP2B = float(np.float32(2 * np.pi - TP2))
    TP2B_s = float(np.float32(np.trunc((2 * np.pi - TP2) * 2 ** 20) / 2 ** 20))
    TP2C = float(np.float32(2 * np.pi - TP2 - TP2B_s))
    INV2PI = float(np.float32(1.0 / (2 * np.pi)))
    LH = float(np.float32(np.log(10000.0) / 32))
    LH1 = float(np.float32(np.trunc(LH * 2 ** 12) / 2 ** 12))
    LH2 = float(np.float32(np.log(10000.0) / 32 - LH1))
    PIH1 = float(np.float32(np.pi / 2))
    PIH2 = float(np.float32(np.pi / 2 - PIH1))

    DMA("sync", x[:].rearrange("p (c t) -> p c t", t=T),
        dI["x_fm"].ap().rearrange("(c p) t -> p c t", p=128),
        r=(), w=("x",))
    DMA("sync", gcols[:, 0:4],
        dI["gvec"].ap().unsqueeze(0).broadcast_to([128, 4]), r=(), w=("gc",))
    for t_, val in ((ones_bf, 1.0), (onesc_bf, 1.0)):
        V_(lambda e, a=t_, v=val: e.memset(a[:], v), r=(), w=("ones",))
    G_(lambda e: e.iota(scrG[:, 0:128], pattern=[[-1, 128]], base=0,
                        channel_multiplier=1,
                        allow_small_or_imprecise_dtypes=True),
       r=(), w=("io1",))
    vts(ident[:], scrG[:, 0:128], 0.0, OP.is_equal, r=("io1",), w=("ident",))
    vts(identf[:], scrG[:, 0:128], 0.0, OP.is_equal, r=("io1",), w=("identf",))
    G_(lambda e: e.iota(scrG[:, 128:256], pattern=[[1, 128]], base=0,
                        channel_multiplier=-1,
                        allow_small_or_imprecise_dtypes=True),
       r=(), w=("io2",))
    vts(scrG[:, 256:384], scrG[:, 128:256], 0.0, OP.is_ge, r=("io2",), w=("tri",))
    vts(masks[:, 0:128], scrG[:, 256:384], gcols[:, 0:1], OP.max,
        r=("tri", "gc"), w=("mask0",))
    vts(masks[:, 128:256], scrG[:, 256:384], gcols[:, 0:1], OP.mult,
        r=("tri", "gc"), w=("mask1",))
    V_(lambda e: e.memset(small[0:1, 1023:1024], 1.0), r=(), w=("onef",))
    V_(lambda e: e.memset(small[:, C_R75:C_R75 + 1], 7.5), r=(), w=("c75",))
    p.emit("vector", lambda e: e.reciprocal(out=small[:, C_R75:C_R75 + 1],
                                            in_=small[:, C_R75:C_R75 + 1]),
           ("c75",), ("r75",), "op")
    V_(lambda e: e.memset(small[32:33, 1023:1024], 1.0), r=(), w=("onef",))
    DMA("sync", small[:, 0:64], dI["b1c"].ap(), r=(), w=("b1c",))
    DMA("sync", small[:, 64:128], dI["b3c"].ap(), r=(), w=("b3c",))
    DMA("sync", small[:, 128:192], dI["cwc"].ap(), r=(), w=("cwc",))
    DMA("sync", small[:, 192:208], dI["cbc"].ap(), r=(), w=("cbc",))
    DMA("sync", small[:, 208:224], dI["bdtc"].ap(), r=(), w=("bdtc",))
    DMA("sync", small[:, 224:352], dI["alc"].ap(), r=(), w=("alraw",))
    DMA("sync", small[:, 480:488], dI["dpc"].ap(), r=(), w=("dpc",))
    DMA("sync", small[0:16, 504:505], dI["ebias"].ap().unsqueeze(1),
        r=(), w=("ebc",))
    DMA("sync", small[:, 488:496], dI["sb1c"].ap(), r=(), w=("sb1c",))
    DMA("sync", small[:, 496:504], dI["sb3c"].ap(), r=(), w=("sb3c",))
    act(small[:, 224:352], small[:, 224:352], AF.Exp, r=("alraw",), w=("alx",))
    vts(small[:, 224:352], small[:, 224:352], -1.0, OP.mult, r=("alx",),
        w=("acol",))
    DMA("gpsimd", brow[0:1, 0:1024], dI["l0b"].ap().unsqueeze(0),
        r=(), w=("l0br",))
    DMA("gpsimd", brow[0:1, 1024:2048], dI["sb2"].ap().unsqueeze(0),
        r=(), w=("sb2r",))
    rowsall_bf = rowsall.ap().bitcast(BF16)

    p.barrier()
    DMA("sync", slotA_f[:, 0:8192].rearrange("p (c q) -> p c q", q=1024),
        dI["l0w"].ap().rearrange("(c p) q -> p c q", p=128),
        r=(), w=("wraw",))

    def quant_tile(src, dst_bf, rk, wk, sc0=528, sc1=560):
        mx = small[:, sc0:sc0 + 32]
        s_ = small[:, sc1:sc1 + 32]
        red(mx, src.rearrange("p (b k) -> p b k", k=32), AX.X, OP.max,
            absval=True, r=rk, w=("q_mx",))
        vts(s_, mx, 7.5, OP.divide, 1e-30, OP.max, r=("q_mx",), w=("q_s",))
        sb3 = s_.unsqueeze(2).broadcast_to([128, 32, 32])
        y = scrG[:, 1024:2048]
        vtt(y.rearrange("p (b k) -> p b k", k=32),
            src.rearrange("p (b k) -> p b k", k=32), sb3, OP.divide,
            r=rk + ("q_s",), w=("q_y",))
        a = scrG[:, 0:1024]
        act(a, y, AF.Abs, r=("q_y",), w=("q_a",))
        ge2, ge4, qi, qv = (scrF[:, 0:1024], scrF[:, 1024:2048],
                            scrF[:, 2048:3072], scrF[:, 3072:4096])
        vts(ge2, a, 2.0, OP.is_ge, r=("q_a",), w=("q_g2",))
        vts(ge4, a, 4.0, OP.is_ge, r=("q_a",), w=("q_g4",))
        vts(qi, ge2, -4.0, OP.mult, 8.0, OP.add, r=("q_g2",), w=("q_qi",))
        vstt(qi, ge4, -2.0, qi, OP.mult, OP.add, r=("q_g4", "q_qi"), w=("q_qi",))
        vts(qv, ge2, 0.125, OP.mult, 0.125, OP.add, r=("q_g2",), w=("q_qv",))
        vstt(qv, ge4, 0.25, qv, OP.mult, OP.add, r=("q_g4", "q_qv"), w=("q_qv",))
        z = scrF[:, 0:1024]
        vtt(z, y, qi, OP.mult, r=("q_y", "q_qi"), w=("q_z",))
        m_ = scrF[:, 1024:2048]
        vts(m_, z, 0.5, OP.add, 1.0, OP.mod, r=("q_z",), w=("q_m",))
        k_ = scrG[:, 0:1024]
        vstt(k_, z, 0.5, m_, OP.add, OP.subtract, r=("q_z", "q_m"), w=("q_k",))
        xdn = scrF[:, 0:1024]
        vtt(xdn, k_, qv, OP.mult, r=("q_k", "q_qv"), w=("q_xdn",))
        vtt(dst_bf.rearrange("p (b k) -> p b k", k=32),
            xdn.rearrange("p (b k) -> p b k", k=32), sb3, OP.mult,
            r=("q_xdn", "q_s"), w=wk)

    def prep_wquant():
        for ce in range(8):
            quant_tile(slotA_f[:, ce * 1024:(ce + 1) * 1024], ("wraw",),
                       slotB[:, ce * 1024:(ce + 1) * 1024], ("wdtm%d" % ce,))
        pmt = pm[0].ap().bitcast(BF16)
        for cd in range(8):
            for ce2 in range(8):
                k = (cd * 8 + ce2) % 2
                tpose(pmt[:, k * 128:(k + 1) * 128],
                      slotB[:, ce2 * 1024 + cd * 128:
                            ce2 * 1024 + (cd + 1) * 128], ident[:],
                      r=("wdtm%d" % ce2, "ident"), w=("pm0",))
                vcopy(wdT[:, cd * 1024 + ce2 * 128:
                          cd * 1024 + (ce2 + 1) * 128],
                      pmt[:, k * 128:(k + 1) * 128], r=("pm0",),
                      w=("wdT", "pm0"))
        p.barrier()

    ln_ct = [0]
    sl16 = lambda c0: small[:, c0:c0 + 16]

    def ln_stats(want_rows):
        ln_ct[0] += 1
        u = "L%d" % ln_ct[0]
        vcopy(slotA[:], x[:], r=("x",), w=("slotA.xb",))
        vtt(slotB[:], x[:], x[:], OP.mult, r=("x",), w=("slotB.xsq",))
        for which, src, kk in ((0, slotA, "slotA.xb"), (1, slotB, "slotB.xsq")):
            for q in range(4):
                pmi = (4 + q) if which == 0 else q
                for c in range(8):
                    mm(pm[pmi][0:1, 0:512], onesc_bf[:],
                       src[:, c * 2048 + q * 512: c * 2048 + (q + 1) * 512],
                       start=(c == 0), stop=(c == 7),
                       r=(kk, "ones"), w=("pm%d" % pmi,))
        for q in range(4):
            vts(rowsall[0:1, q * 512:(q + 1) * 512],
                pm[4 + q][0:1, 0:512], 1.0 / D, OP.mult,
                r=("pm%d" % (4 + q),), w=("mrow" + u, "pm%d" % (4 + q)))
            vts(rowsall[32:33, q * 512:(q + 1) * 512],
                pm[q][0:1, 0:512], 1.0 / D, OP.mult,
                r=("pm%d" % q,), w=("sqrow" + u, "pm%d" % q))
        for which2, rsrc, dst_c, kk2 in (
                (0, rowsall[0:1, :], C_MS, "mtm"), 
                (1, rowsall[32:33, :], C_SQ, "sqtm")):
            pmi2 = 4 + which2
            for t2 in range(16):
                tpose(pm[pmi2][:, t2:t2 + 1],
                      rsrc[:, t2 * 128:(t2 + 1) * 128],
                      small[32 * which2:32 * which2 + 1, 1023:1024],
                      r=(("mrow" if which2 == 0 else "sqrow") + u, "onef"),
                      w=("pm%d" % pmi2,))
            vcopy(small[:, dst_c:dst_c + 16], pm[pmi2][:, 0:16],
                  r=("pm%d" % pmi2,), w=(kk2 + u, "pm%d" % pmi2))
        vcopy(sl16(C_MC), sl16(C_MS), r=("mtm" + u,), w=("mc" + u,))
        vtt(sl16(C_T1), sl16(C_MS), sl16(C_MS), OP.mult, r=("mtm" + u,),
            w=("m2" + u,))
        vtt(sl16(C_VAR), sl16(C_SQ), sl16(C_T1), OP.subtract,
            r=("sqtm" + u, "m2" + u), w=("v0" + u,))
        vts(sl16(C_VAR), sl16(C_VAR), 1e-5, OP.add, r=("v0" + u,), w=("vE" + u,))
        p.emit("vector", lambda e: e.reciprocal(out=sl16(C_INV), in_=sl16(C_VAR)),
               ("vE" + u,), ("iv" + u,), "op")
        act(sl16(C_SC), sl16(C_INV), AF.Sqrt, r=("iv" + u,), w=("s0" + u,))
        vtt(sl16(C_T1), sl16(C_SC), sl16(C_SC), OP.mult, r=("s0" + u,),
            w=("nt1" + u,))
        vtt(sl16(C_T2), sl16(C_T1), sl16(C_VAR), OP.mult,
            r=("nt1" + u, "vE" + u), w=("nt2" + u,))
        vts(sl16(C_T3), sl16(C_T2), -0.5, OP.mult, 1.5, OP.add,
            r=("nt2" + u,), w=("nt3" + u,))
        vtt(sl16(C_SC), sl16(C_SC), sl16(C_T3), OP.mult,
            r=("s0" + u, "nt3" + u), w=("sc" + u,))
        if want_rows:
            for which2, src_c, kk2, kk3 in (
                    (0, C_MC, "mc", "mrowF"), (1, C_SC, "sc", "srowF")):
                for t2 in range(16):
                    q2 = t2 // 4
                    tpose(pm[q2][0:1, (t2 % 4) * 128:(t2 % 4 + 1) * 128],
                          small[:, src_c + t2:src_c + t2 + 1], identf[:],
                          r=(kk2 + u, "identf"), w=("pm%d" % q2,))
                for q2 in range(4):
                    vts(rowsall[32 * which2:32 * which2 + 1,
                                q2 * 512:(q2 + 1) * 512],
                        pm[q2][0:1, 0:512], 1.0, OP.mult, None, None,
                        r=("pm%d" % q2,), w=(kk3 + u, "pm%d" % q2))
        return u

    def ln_fm_xln(u):
        DMA("scalar", ln_dr.ap()[0:1, :], rowsall.ap()[0:1, :],
            r=("mrowF" + u,), w=("lndr0",))
        DMA("scalar", ln_dr.ap()[1:2, :], rowsall.ap()[32:33, :],
            r=("srowF" + u,), w=("lndr1",))
        DMA("scalar", scrF[:, 0:2048],
            ln_dr.ap()[0].unsqueeze(0).broadcast_to([128, 2048]),
            r=("lndr0",), w=("scrF.mb",))
        DMA("scalar", scrF[:, 2048:4096],
            ln_dr.ap()[1].unsqueeze(0).broadcast_to([128, 2048]),
            r=("lndr1",), w=("scrF.sb",))
        for c in range(8):
            for hh in range(2):
                sl_ = slice(c * 2048 + hh * 1024, c * 2048 + (hh + 1) * 1024)
                bsl = slice(hh * 1024, (hh + 1) * 1024)
                tb = scrG[:, 0:1024] if hh == 0 else scrG[:, 1024:2048]
                gk = "scrG.a" if hh == 0 else "scrG.b"
                vtt(tb, x[:, sl_], scrF[:, 0:2048][:, bsl], OP.subtract,
                    r=("x", "scrF.mb"), w=(gk,))
                vtt(slotA[:, sl_], tb, scrF[:, 2048:4096][:, bsl], OP.mult,
                    r=(gk, "scrF.sb"),
                    w=("xln_%d" % c, "slotA.xb"))

    def quant_tile(src, src_keys, dst_bf, dst_keys):
        mx = small[:, C_QMX:C_QMX + 32]
        s_ = small[:, C_QS:C_QS + 32]
        red(mx, src.rearrange("p (b k) -> p b k", k=32), AX.X, OP.max,
            absval=True, r=src_keys, w=("q_mx",))
        vts(s_, mx, small[:, C_R75:C_R75 + 1], OP.mult, 1e-30, OP.max,
            r=("q_mx", "r75"), w=("q_s",))
        iv = small[:, C_QIV:C_QIV + 32]
        p.emit("vector", lambda e: e.reciprocal(out=iv, in_=s_),
               ("q_s",), ("q_iv",), "op")
        ivb = iv.unsqueeze(2).broadcast_to([128, 32, 32])
        sb3 = s_.unsqueeze(2).broadcast_to([128, 32, 32])
        y = scrG[:, 1024:2048]
        vtt(y.rearrange("p (b k) -> p b k", k=32),
            src.rearrange("p (b k) -> p b k", k=32), ivb, OP.mult,
            r=src_keys + ("q_iv",), w=("scrG.b",))
        a = scrG[:, 0:1024]
        act(a, y, AF.Abs, r=("scrG.b",), w=("scrG.a",) + src_keys)
        ge2, ge4, qi, qv = (scrF[:, 0:1024], scrF[:, 1024:2048],
                            scrF[:, 2048:3072], scrF[:, 3072:4096])
        vts(ge2, a, 2.0, OP.is_ge, None, None, r=("scrG.a",), w=("scrF.q0",))
        vts(ge4, a, 4.0, OP.is_ge, None, None, r=("scrG.a",), w=("scrF.q1",))
        vts(qi, ge2, -4.0, OP.mult, 8.0, OP.add, r=("scrF.q0",), w=("scrF.q2",))
        vstt(qi, ge4, -2.0, qi, OP.mult, OP.add,
             r=("scrF.q1", "scrF.q2"), w=("scrF.q2",))
        vts(qv, ge2, 0.125, OP.mult, 0.125, OP.add, r=("scrF.q0",),
            w=("scrF.q3",))
        vstt(qv, ge4, 0.25, qv, OP.mult, OP.add,
             r=("scrF.q1", "scrF.q3"), w=("scrF.q3",))
        z = scrG[:, 0:1024]
        vtt(z, y, qi, OP.mult, r=("scrG.b", "scrF.q2"), w=("scrG.a",))
        act(scrG_i[:, 1024:2048], z, AF.Identity, r=("scrG.a",),
            w=("scrG.b",))
        t1 = scrF[:, 0:1024]
        vtt(t1, scrG_i[:, 1024:2048], qv, OP.mult, r=("scrG.b", "scrF.q3"),
            w=("scrF.q0",))
        vtt(dst_bf.rearrange("p (b k) -> p b k", k=32),
            t1.rearrange("p (b k) -> p b k", k=32), sb3, OP.mult,
            r=("scrF.q0", "q_s"), w=dst_keys)

    def l0ffn(app):
        u = ln_stats(want_rows=False)
        for t_ in range(NB):
            for c in range(8):
                k = c % 2
                tpose(pm[k][:, 0:128],
                      x[:, c * 2048 + t_ * 128: c * 2048 + (t_ + 1) * 128],
                      identf[:],
                      r=("x", "identf"), w=("pm%d" % k,))
                vts(scrG[:, c * 128:(c + 1) * 128], pm[k][:, 0:128],
                    small[:, C_MC + t_:C_MC + t_ + 1], OP.subtract,
                    small[:, C_SC + t_:C_SC + t_ + 1], OP.mult,
                    r=("pm%d" % k, "mc" + u, "sc" + u),
                    w=("scrG.a", "pm%d" % k))
            quant_tile(scrG[:, 0:1024], ("scrG.a",),
                       qtm[:, 0:1024], ("qtm",))
            pmtb = pm[2].ap().bitcast(BF16)
            for c in range(8):
                k = c % 2
                tpose(pmtb[:, k * 128:(k + 1) * 128],
                      qtm[:, c * 128:(c + 1) * 128], ident[:],
                      r=("qtm", "ident"), w=("pm2",))
                vcopy(slotB[:, c * 2048 + t_ * 128: c * 2048 + (t_ + 1) * 128],
                      pmtb[:, k * 128:(k + 1) * 128],
                      r=("pm2",), w=("xd%d_%d" % (c, t_), "slotB.xsq", "pm2"))
        for ce in range(8):
            for tc in range(4):
                pk = "pm%d" % (4 + tc % 2)
                pmm = pm[4 + tc % 2]
                for ck in range(8):
                    mm(pmm[:, 0:512],
                       wdT[:, ck * 1024 + ce * 128: ck * 1024 + (ce + 1) * 128],
                       slotB[:, ck * 2048 + tc * 512: ck * 2048 + (tc + 1) * 512],
                       start=(ck == 0), stop=False,
                       r=("wdT",) + tuple("xd%d_%d" % (ck, tt2)
                                          for tt2 in range(tc * 4, tc * 4 + 4)),
                       w=(pk,))
                mm(pmm[:, 0:512], brow[0:1, ce * 128:(ce + 1) * 128],
                   ones_bf[:], start=False, stop=True, r=("l0br",), w=(pk,))
                vstt(x[:, ce * 2048 + tc * 512: ce * 2048 + (tc + 1) * 512],
                     pmm[:, 0:512], 0.5,
                     x[:, ce * 2048 + tc * 512: ce * 2048 + (tc + 1) * 512],
                     OP.mult, OP.add, r=(pk, "x"), w=("x", pk))
        p.barrier()

    prep_wquant()

    def dbg_dump(src_ap, nelem, rkeys):
        o_ = (out_fm.ap().bitcast(BF16).rearrange("a b -> (a b)")
              .rearrange("(p f) -> p f", p=src_ap.shape[0]))
        DMA("sync", o_[:, 0:src_ap.shape[1]], src_ap, r=rkeys,
            w=("outw", "x"))

    if not SKIP_P1:
        l0ffn(1)

    def attention():
        G_(lambda e: e.iota(scrG[0:1, 0:32], pattern=[[1, 32]], base=0,
                            channel_multiplier=0,
                            allow_small_or_imprecise_dtypes=True),
           r=(), w=("scrG.a",))
        vts(scrG[0:1, 32:64], scrG[0:1, 0:32], -LH1, OP.mult, None, None,
            r=("scrG.a",), w=("scrG.a",))
        vstt(scrG[0:1, 64:96], scrG[0:1, 0:32], -LH2, scrG[0:1, 32:64],
             OP.mult, OP.add, r=("scrG.a",), w=("scrG.a",))
        act(scrG[0:1, 96:128], scrG[0:1, 64:96], AF.Exp,
            r=("scrG.a",), w=("invf",))
        G_(lambda e: e.iota(rowsall[0:1, 0:2048], pattern=[[256, 16], [1, 128]],
                            base=0, channel_multiplier=0,
                            allow_small_or_imprecise_dtypes=True),
           r=(), w=("tg0",))
        vts(rowsall[0:1, 0:2048], rowsall[0:1, 0:2048], gcols[0:1, 2:3],
            OP.add, None, None, r=("tg0", "gc"), w=("tgrow",))
        for q in range(4):
            mm(pm[q][0:32, 0:512], scrG[0:1, 96:128],
               rowsall[0:1, q * 512:(q + 1) * 512], start=True, stop=True,
               r=("invf", "tgrow"), w=("pm%d" % q,))
        for q in range(4):
            pk = "pm%d" % q
            th = pm[q][0:32, 0:512]
            r1 = scrG[0:32, 0:512]
            ki = scrG_i[0:32, 512:1024]
            ph = scrG[0:32, 1024:1536]
            wr = scrG[0:32, 1536:2048]
            for which, dst_base, shift in ((0, 64, 0.0), (1, 96, 0.25)):
                vts(r1, th, INV2PI, OP.mult, shift, OP.add,
                    r=(pk,), w=("scrG.a",))
                act(ki, r1, AF.Identity, r=("scrG.a",), w=("scrG.b",))
                vstt(ph, ki, -TP2, th, OP.mult, OP.add,
                     r=("scrG.b", pk), w=("scrG.b2",))
                vstt(ph, ki, -TP2B_s, ph, OP.mult, OP.add,
                     r=("scrG.b", "scrG.b2"), w=("scrG.b2",))
                vstt(ph, ki, -TP2C, ph, OP.mult, OP.add,
                     r=("scrG.b", "scrG.b2"), w=("scrG.b2",))
                if which == 1:
                    vts(ph, ph, PIH1, OP.add, None, None,
                        r=("scrG.b2",), w=("scrG.b2",))
                    vts(ph, ph, PIH2, OP.add, None, None,
                        r=("scrG.b2",), w=("scrG.b2",))
                vts(wr, ph, float(np.float32(np.pi)), OP.is_gt, None, None,
                    r=("scrG.b2",), w=("scrG.w",))
                vstt(ph, wr, -float(np.float32(2 * np.pi)), ph,
                     OP.mult, OP.add, r=("scrG.w", "scrG.b2"), w=("scrG.b2",))
                act(rowsall[dst_base:dst_base + 32, q * 512:(q + 1) * 512],
                    ph, AF.Sin, r=("scrG.b2",),
                    w=(("sintab",) if which == 0 else ("costab",), pk))
        u = ln_stats(want_rows=True)
        ln_fm_xln(u)
        for kk2 in range(2):
            for a2 in range(2):
                for b2 in range(2):
                    DMA("scalar", rope_dr.ap()[kk2, a2, :, b2, :],
                        rowsall.ap()[64 + 32 * kk2:96 + 32 * kk2, :],
                        r=("sintab" if kk2 == 0 else "costab",),
                        w=("ropedr%d" % kk2,))
        DMA("scalar", scrF[:, 0:2048], rope_dr.ap()[0],
            r=("ropedr0",), w=("scrF.mb", "scrF.srep"))
        DMA("scalar", scrF[:, 2048:4096], rope_dr.ap()[1],
            r=("ropedr1",), w=("scrF.sb", "scrF.crep"))
        G_(lambda e: e.iota(scrG_i[:, 0:1], pattern=[[0, 1]], base=0,
                            channel_multiplier=1), r=(), w=("scrG.a",))
        p.emit("vector", lambda e: e.tensor_scalar(
            out=scrG_i[:, 1:2], in0=scrG_i[:, 0:1], scalar1=1, scalar2=None,
            op0=OP.bitwise_and), ("scrG.a",), ("scrG.a",), "op")
        vts(small[:, C_SGN:C_SGN + 1], scrG_i[:, 1:2], 2.0, OP.mult,
            -1.0, OP.add, r=("scrG.a",), w=("sgnc",))
        vts(scrF[:, 0:2048], scrF[:, 0:2048], small[:, C_SGN:C_SGN + 1],
            OP.mult, None, None, r=("scrF.srep", "sgnc"), w=("scrF.srep",))

        def load_w(name, kpref):
            for c in range(8):
                DMA("gpsimd", wA[:, c * 1024:(c + 1) * 1024],
                    dI[name].ap()[c * 128:(c + 1) * 128, :],
                    r=(), w=("%s%d" % (kpref, c), "wA.%d" % c))

        load_w("wv_t", "wv")
        for t_ in range(NB):
            pv0 = 4 + (t_ % 2) * 2
            for ck in range(8):
                for hf in range(2):
                    mm(pm[pv0 + hf][:, 0:512],
                       slotA[:, ck * 2048 + t_ * 128: ck * 2048 + (t_ + 1) * 128],
                       wA[:, ck * 1024 + hf * 512: ck * 1024 + (hf + 1) * 512],
                       start=(ck == 0), stop=(ck == 7),
                       r=("xln_%d" % ck, "wv%d" % ck),
                       w=("pm%d" % (pv0 + hf),))
            for hf in range(2):
                vcopy(slotB[:, t_ * 1024 + hf * 512: t_ * 1024 + (hf + 1) * 512],
                      pm[pv0 + hf][:, 0:512],
                      r=("pm%d" % (pv0 + hf),),
                      w=("vtm%d" % t_, "slotB.v", "pm%d" % (pv0 + hf)))
        DMA("gpsimd", kv_agin.ap()[1], slotB[:],
            r=tuple("vtm%d" % t_ for t_ in range(NB)), w=("agin.v",))

        load_w("wk_t", "wk")
        def rope_chunk(ce, tc, wpref, dst_bf, dkeys, pk, pmm):
            t0 = scrG[:, 0:512]
            t1 = scrG[:, 512:1024]
            t2 = scrG[:, 1024:1536]
            vcopy(t0, pmm[:, 0:512], r=(pk,), w=("scrG.a", pk))
            p.emit("vector", lambda e, a=t0, b=t1: e.stream_shuffle(
                out=b, in_=a, mask=SWAPM), ("scrG.a",), ("scrG.b",), "op")
            vtt(t2, t0, scrF[:, 2048 + tc * 512:2048 + (tc + 1) * 512],
                OP.mult, r=("scrG.a", "scrF.crep"), w=("scrG.b2",))
            vtt(t1, t1, scrF[:, tc * 512:(tc + 1) * 512], OP.mult,
                r=("scrG.b", "scrF.srep"), w=("scrG.b",))
            vtt(dst_bf, t2, t1, OP.add, r=("scrG.b2", "scrG.b"), w=dkeys)
        for ce in range(8):
            for tc in range(4):
                pk = "pm%d" % (tc % 2)
                pmm = pm[tc % 2]
                for ck in range(8):
                    mm(pmm[:, 0:512],
                       wA[:, ck * 1024 + ce * 128: ck * 1024 + (ce + 1) * 128],
                       slotA[:, ck * 2048 + tc * 512: ck * 2048 + (tc + 1) * 512],
                       start=(ck == 0), stop=(ck == 7),
                       r=("xln_%d" % ck, "wk%d" % ck), w=(pk,))
                dst = qtm[:, (tc % 2) * 512:(tc % 2 + 1) * 512]
                rope_chunk(ce, tc, "wk", dst, ("qtm.%d" % (tc % 2),), pk, pmm)
                DMA("gpsimd", kv_agin.ap()[0][:, ce * 2048 + tc * 512:
                                              ce * 2048 + (tc + 1) * 512],
                    dst, r=("qtm.%d" % (tc % 2),), w=("agin.k%d%d" % (ce, tc),))
        CC("AllGather", OP.bypass, [kv_agin.ap()], [kv_agout.ap()],
           r=("agin.v",) + tuple("agin.k%d%d" % (ce, tc)
                                 for ce in range(8) for tc in range(4)),
           w=("agout",))
        load_w("wq_t", "wq")
        for ce in range(8):
            for tc in range(4):
                pk = "pm%d" % (2 + tc % 2)
                pmm = pm[2 + tc % 2]
                for ck in range(8):
                    mm(pmm[:, 0:512],
                       wA[:, ck * 1024 + ce * 128: ck * 1024 + (ce + 1) * 128],
                       slotA[:, ck * 2048 + tc * 512: ck * 2048 + (tc + 1) * 512],
                       start=(ck == 0), stop=(ck == 7),
                       r=("xln_%d" % ck, "wq%d" % ck), w=(pk,))
                rope_chunk(ce, tc, "wq",
                           slotB[:, ce * 2048 + tc * 512:
                                 ce * 2048 + (tc + 1) * 512],
                           ("q_%d" % ce, "slotB.v", "agin.v"), pk, pmm)
        if DBG == "q":
            dbg_dump(slotB[:], 128 * 16384,
                     tuple("q_%d" % c for c in range(8)))
        if DBG == "xln":
            dbg_dump(slotA[:], 128 * 16384,
                     tuple("xln_%d" % c for c in range(8)))
        if ATT_SUB < 2:
            p.barrier()
            return
        V_(lambda e: e.memset(scrF_bf[:, 0:4160], 1.0),
           r=(), w=("vg0", "vg1", "scrF.att", "scrF.srep", "scrF.crep"))
        p.emit("vector", lambda e: e.nop(), (),
               tuple("xln_%d" % c for c in range(8)) + ("xlndead",), "op")
        p.barrier()
        qst = scrG_bf[0:64, 2048:4096]
        for h in range(ATT_HEADS):
            kb, vb = h % 2, h % 2
            kk = "wAk%d" % kb
            vk = "vg%d" % vb
            kg = wA.ap()[0:64, kb * 4096:(kb + 1) * 4096]
            hp = (h % 2) * 64
            for g2 in range(2):
                DMA("sync", kg.rearrange("p (j i) -> p j i", i=256)[:, :,
                                                                   g2 * 128:
                                                                   (g2 + 1) * 128],
                    kv_agout.ap()[g2, 0, hp:hp + 64,
                                  (h // 2) * 2048:(h // 2 + 1) * 2048]
                    .rearrange("p (j i) -> p j i", i=128),
                    r=("agout",), w=(kk,))
            vg = scrF_bf[:, vb * 2080:(vb + 1) * 2080]
            for g2 in range(2):
                DMA("sync", vg.rearrange("p (b e) -> p b e", e=130)[:, :,
                                                                   g2 * 65:
                                                                   g2 * 65 + 64],
                    kv_agout.ap()[g2, 1, :, :]
                    .rearrange("p (j e) -> p j e", e=1024)[:, :, h * 64:
                                                           (h + 1) * 64],
                    r=("agout",), w=(vk,))
            if hp == 0:
                qsrc = slotB.ap()[0:64, (h // 2) * 2048:(h // 2 + 1) * 2048]
                qkey = "q_%d" % (h // 2)
            else:
                vcopy(qst, slotB[hp:hp + 64, (h // 2) * 2048:(h // 2 + 1) * 2048],
                      r=("q_%d" % (h // 2),), w=("qst",))
                qsrc, qkey = qst, "qst"
            cnt = [0]

            def emit_pv(B):
                J = B >> 1
                pT = (scrF_bf[:, 4160:6208] if B % 2 == 0
                      else scrG_bf[:, 0:2048])
                pTk = "pT%d" % (B % 2)
                for j in range(J, NB):
                    opk = "pm%d" % (4 + j // 4)
                    first_b = (B == 0 and j % 4 == 0)
                    last_b = (j % 4 == 3 and B == 2 * j + 1)
                    if h % 2 == 0:
                        mm(pm[4 + j // 4][0:65, (j % 4) * 128:(j % 4 + 1) * 128],
                           vg[:, B * 65:(B + 1) * 65],
                           pT[:, (j - J) * 128:(j - J + 1) * 128],
                           start=first_b, stop=last_b,
                           r=(vk, pTk), w=(opk,))
                    else:
                        mm(pm[4 + j // 4][64:128, (j % 4) * 128:(j % 4 + 1) * 128],
                           vg[:, B * 65:B * 65 + 64],
                           pT[:, (j - J) * 128:(j - J + 1) * 128],
                           start=first_b, stop=last_b,
                           r=(vk, pTk), w=(opk,))
                        mm(pm[4 + j // 4][0:1, (j % 4) * 128:(j % 4 + 1) * 128],
                           onesc_bf[:],
                           pT[:, (j - J) * 128:(j - J + 1) * 128],
                           start=first_b, stop=last_b,
                           r=("ones", pTk), w=(opk,))

            for B in range(32):
                J = B >> 1
                Wd = T - J * 128
                pT = (scrF_bf[:, 4160:6208] if B % 2 == 0
                      else scrG_bf[:, 0:2048])
                pTk = "pT%d" % (B % 2)
                nch = (Wd + 511) // 512
                for q5 in range(nch):
                    w5 = min(512, Wd - q5 * 512)
                    spk = "pm%d" % (cnt[0] % 2)
                    spm = pm[cnt[0] % 2]
                    cnt[0] += 1
                    mm(spm[:, 0:w5], kg[:, B * 128:(B + 1) * 128],
                       qsrc[:, J * 128 + q5 * 512: J * 128 + q5 * 512 + w5],
                       start=True, stop=True,
                       r=(kk, qkey), w=(spk,))
                    act(pT[:, q5 * 512:q5 * 512 + w5], spm[:, 0:w5], AF.Exp,
                        scale=0.125, r=(spk,), w=(pTk, spk))
                gph = B & 1
                vtt(pT[:, 0:128], pT[:, 0:128],
                    masks[:, gph * 128:(gph + 1) * 128], OP.mult,
                    r=(pTk, "mask%d" % gph), w=(pTk,))
                if B > 0:
                    emit_pv(B - 1)
            emit_pv(31)
            dp = 64 if h % 2 == 0 else 0
            qtm_f = qtm.ap().bitcast(F32)
            for j4 in range(4):
                opk = "pm%d" % (4 + j4)
                vcopy(slotA[hp:hp + 64, (h // 2) * 2048 + j4 * 512:
                            (h // 2) * 2048 + (j4 + 1) * 512],
                      pm[4 + j4][hp:hp + 64, 0:512],
                      r=(opk, "xlndead"), w=("oall%d" % (h // 2), "slotA.xb"))
                vcopy(qtm_f[dp:dp + 1, 0:512], pm[4 + j4][dp:dp + 1, 0:512],
                      r=(opk,), w=("dstage", opk))
                DMA("scalar", rowsall.ap()[64 + h:65 + h,
                                           j4 * 512:(j4 + 1) * 512],
                    qtm_f[dp:dp + 1, 0:512],
                    r=("dstage",), w=("drow%d" % h, "sintab", "dstage"))
        if DBG == "oall":
            dbg_dump(slotA[:], 128 * 16384,
                     tuple("oall%d" % c for c in range(8)))
        if DBG == "dinv":
            dbg_dump(rowsall.ap().bitcast(BF16)[64:80, :], 16 * 4096,
                tuple("drow%d" % hh for hh in range(ATT_HEADS)))
        if ATT_SUB < 3:
            p.barrier()
            return
        p.barrier()
        p.emit("vector", lambda e: e.reciprocal(out=rowsall[64:80, :],
                                                in_=rowsall[64:80, :]),
               tuple("drow%d" % h for h in range(ATT_HEADS)), ("dinv",), "op")
        DMA("scalar", d_dr.ap(), rowsall.ap()[64:80, :],
            r=("dinv",), w=("ddr",))
        for ce in range(8):
            DMA("scalar", scrG[:, 0:2048],
                d_dr.ap()[2 * ce:2 * ce + 2].unsqueeze(1)
                .broadcast_to([2, 64, 2048]),
                r=("ddr",) + (("osc%d" % (ce - 1),) if ce > 0 else ()),
                w=("dbc",))
            vtt(slotA[:, ce * 2048:(ce + 1) * 2048],
                slotA[:, ce * 2048:(ce + 1) * 2048], scrG[:, 0:2048],
                OP.mult, r=("dbc", "oall%d" % ce), w=("osc%d" % ce,))
        load_w("wo_t", "wo")
        for ce in range(8):
            for tc in range(4):
                pk = "pm%d" % (tc % 2)
                pmm = pm[tc % 2]
                for ck in range(8):
                    mm(pmm[:, 0:512],
                       wA[:, ck * 1024 + ce * 128: ck * 1024 + (ce + 1) * 128],
                       slotA[:, ck * 2048 + tc * 512: ck * 2048 + (tc + 1) * 512],
                       start=(ck == 0), stop=(ck == 7),
                       r=("osc%d" % ck, "wo%d" % ck), w=(pk,))
                vtt(x[:, ce * 2048 + tc * 512: ce * 2048 + (tc + 1) * 512],
                    x[:, ce * 2048 + tc * 512: ce * 2048 + (tc + 1) * 512],
                    pmm[:, 0:512], OP.add, r=(pk, "x"), w=("x", pk))
        p.barrier()

    if STOP_AFTER >= 2:
        attention()

    if STOP_AFTER >= 3:
        l0ffn(2)


    def moe(app):
        u = ln_stats(want_rows=True)
        ln_fm_xln(u)
        a_ = "M%d" % app
        DMA("gpsimd", qtm[:, 0:128].rearrange("p (c e) -> p c e", e=16),
            dI["gate_wt"].ap().rearrange("(c p) e -> p c e", p=128),
            r=(), w=("gwt", "qtm.0"))
        for q in range(4):
            pk = "pm%d" % q
            for ck in range(8):
                mm(pm[q][0:16, 0:512], qtm[:, ck * 16:(ck + 1) * 16],
                   slotA[:, ck * 2048 + q * 512: ck * 2048 + (q + 1) * 512],
                   start=(ck == 0), stop=(ck == 7),
                   r=("gwt", "xln_%d" % ck), w=(pk,))
            vts(scrG[0:16, q * 512:(q + 1) * 512], pm[q][0:16, 0:512],
                small[0:16, 504:505], OP.add, None, None,
                r=(pk, "ebc"), w=("gbia" + a_,))
            act(scrF[0:16, 2048 + q * 512:2048 + (q + 1) * 512],
                pm[q][0:16, 0:512],
                AF.Sigmoid, r=(pk,), w=("gsco" + a_, pk))
        for t_ in range(NB):
            tpose(pm[4][:, t_ * 16:(t_ + 1) * 16],
                  scrG[0:16, t_ * 128:(t_ + 1) * 128], identf[0:16, 0:16],
                  r=("gbia" + a_, "identf"), w=("pm4",))
            tpose(pm[5][:, t_ * 16:(t_ + 1) * 16],
                  scrF[0:16, 2048 + t_ * 128:2048 + (t_ + 1) * 128],
                  identf[0:16, 0:16],
                  r=("gsco" + a_, "identf"), w=("pm5",))
        bia3 = pm[4].ap()[:, 0:256].rearrange("p (t e) -> p t e", e=16)
        sco3 = pm[5].ap()[:, 0:256].rearrange("p (t e) -> p t e", e=16)
        sf3 = lambda c0: scrF[:, c0:c0 + 256].rearrange(
            "p (t e) -> p t e", e=16)
        col3 = lambda c0: scrF[:, c0:c0 + 16].unsqueeze(2).broadcast_to(
            [128, 16, 16])
        red(scrF[:, 1280:1296], bia3, AX.X, OP.max, r=("pm4",),
            w=("gm1" + a_,))
        vtt(sf3(0), bia3, col3(1280), OP.is_ge, r=("pm4", "gm1" + a_),
            w=("scrF.q0",))
        vstt(scrF[:, 256:512], scrF[:, 0:256], -1e30, pm[4].ap()[:, 0:256],
             OP.mult, OP.add, r=("scrF.q0", "pm4"), w=("scrF.q1",))
        red(scrF[:, 1296:1312], sf3(256), AX.X, OP.max, r=("scrF.q1",),
            w=("gm2" + a_,))
        vtt(sf3(512), sf3(256), col3(1296), OP.is_ge,
            r=("scrF.q1", "gm2" + a_), w=("scrF.q2",))
        vtt(scrF[:, 0:256], scrF[:, 0:256], scrF[:, 512:768], OP.add,
            r=("scrF.q0", "scrF.q2"), w=("scrF.q0",))
        vtt(scrF[:, 256:512], scrF[:, 0:256], pm[5].ap()[:, 0:256], OP.mult,
            r=("scrF.q0", "pm5"), w=("scrF.q1", "pm5"))
        red(scrF[:, 1312:1328], sf3(256), AX.X, OP.add, r=("scrF.q1",),
            w=("gws" + a_,))
        vts(scrF[:, 1328:1344], scrF[:, 1312:1328], 1e-10, OP.add, None, None,
            r=("gws" + a_,), w=("gwp" + a_,))
        p.emit("vector", lambda e: e.reciprocal(out=scrF[:, 1344:1360],
                                                in_=scrF[:, 1328:1344]),
               ("gwp" + a_,), ("gwi" + a_,), "op")
        vtt(sf3(512), sf3(256), col3(1344), OP.mult,
            r=("scrF.q1", "gwi" + a_), w=("scrF.q2",))
        for t_ in range(NB):
            k2 = 4 + t_ % 2
            tpose(pm[k2][0:16, 0:128], scrF[:, 512 + t_ * 16:512 + (t_ + 1) * 16],
                  identf[:], r=("scrF.q2", "identf"), w=("pm%d" % k2,))
            vcopy(rowsall_bf[64:80, t_ * 128:(t_ + 1) * 128],
                  pm[k2][0:16, 0:128], r=("pm%d" % k2,),
                  w=("gwfm" + a_, "drow%d" % (t_ % ATT_HEADS), "pm%d" % k2))
        DMA("gpsimd", gw_dr.ap(), rowsall_bf[64:80, 0:2048],
            r=("gwfm" + a_,), w=("gwdr" + a_,))
        DMA("gpsimd", rowsall_bf[64:80, 2048:3072], dI["b2"].ap(),
            r=(), w=("b2sb",))

        def load_expert(e_, buf):
            WT = wA if buf == 0 else wdT
            w2c0 = 0 if buf == 0 else 12288
            DMA("gpsimd",
                WT.ap()[:, 0:4096].rearrange("p (c r) -> p c r", r=512),
                dI["w1t"].ap()[e_].rearrange("(c p) r -> p c r", p=128),
                r=(), w=("w1b%d" % buf,))
            DMA("gpsimd",
                WT.ap()[:, 4096:8192].rearrange("p (c r) -> p c r", r=512),
                dI["w3t"].ap()[e_].rearrange("(c p) r -> p c r", p=128),
                r=(), w=("w3b%d" % buf,))
            DMA("gpsimd",
                slotB_b[:, w2c0:w2c0 + 4096].rearrange(
                    "p (c r) -> p c r", r=1024),
                dI["w2t"].ap()[e_].rearrange("(c p) r -> p c r", p=128),
                r=(), w=("w2b%d" % buf,))

        def load_shared(half, buf):
            WT = wA if buf == 0 else wdT
            w2c0 = 0 if buf == 0 else 12288
            DMA("gpsimd",
                WT.ap()[:, 0:4096].rearrange("p (c r) -> p c r", r=512),
                dI["sw1t"].ap().rearrange("(c p) r -> p c r", p=128)
                [:, :, half * 512:(half + 1) * 512],
                r=(), w=("w1b%d" % buf,))
            DMA("gpsimd",
                WT.ap()[:, 4096:8192].rearrange("p (c r) -> p c r", r=512),
                dI["sw3t"].ap().rearrange("(c p) r -> p c r", p=128)
                [:, :, half * 512:(half + 1) * 512],
                r=(), w=("w3b%d" % buf,))
            DMA("gpsimd",
                slotB_b[:, w2c0:w2c0 + 4096].rearrange(
                    "p (c r) -> p c r", r=1024),
                dI["sw2t"].ap()[half * 512:(half + 1) * 512, :]
                .rearrange("(c p) r -> p c r", p=128),
                r=(), w=("w2b%d" % buf,))

        def gbc_load(e_, buf):
            DMA("scalar", slotB_b[:, 4096 + buf * 2048:4096 + (buf + 1) * 2048],
                gw_dr.ap()[e_].unsqueeze(0).broadcast_to([128, 2048]),
                r=("gwdr" + a_,), w=("gbc%d" % buf,))

        HB = 8192
        for ie in range(18):
            shared = ie < 2
            e_ = ie - 2
            if shared:
                load_shared(ie, ie % 2)
            else:
                load_expert(e_, ie % 2)
                gbc_load(e_, ie % 2)
            hb = ie % 2
            W1 = wA if hb == 0 else wdT
            W2C0 = 0 if hb == 0 else 12288
            for tc in range(4):
                for dffc in range(4):
                    for ck in range(8):
                        mm(pm[dffc][:, 0:512],
                           W1[:, ck * 512 + dffc * 128: ck * 512 + (dffc + 1) * 128],
                           slotA[:, ck * 2048 + tc * 512: ck * 2048 + (tc + 1) * 512],
                           start=(ck == 0), stop=(ck == 7),
                           r=("w1b%d" % hb, "xln_%d" % ck),
                           w=("pm%d" % dffc,))
                        mm(pm[4 + dffc][:, 0:512],
                           W1[:, 4096 + ck * 512 + dffc * 128:
                              4096 + ck * 512 + (dffc + 1) * 128],
                           slotA[:, ck * 2048 + tc * 512: ck * 2048 + (tc + 1) * 512],
                           start=(ck == 0), stop=(ck == 7),
                           r=("w3b%d" % hb, "xln_%d" % ck),
                           w=("pm%d" % (4 + dffc),))
                for dffc in range(4):
                    if shared:
                        b1c_ = small[:, 488 + ie * 4 + dffc:489 + ie * 4 + dffc]
                        b3c_ = small[:, 496 + ie * 4 + dffc:497 + ie * 4 + dffc]
                        bk = ("sb1c", "sb3c")
                    else:
                        b1c_ = small[:, e_ * 4 + dffc:e_ * 4 + dffc + 1]
                        b3c_ = small[:, 64 + e_ * 4 + dffc:65 + e_ * 4 + dffc]
                        bk = ("b1c", "b3c")
                    act(scrG[:, dffc * 512:(dffc + 1) * 512], pm[dffc][:, 0:512],
                        AF.Silu, bias=b1c_, r=("pm%d" % dffc, bk[0]),
                        w=("scrG.a", "pm%d" % dffc))
                    tgt = slotB_b[:, HB + hb * 2048 + dffc * 512:
                                  HB + hb * 2048 + (dffc + 1) * 512]
                    vstt(tgt, pm[4 + dffc][:, 0:512], b3c_,
                         scrG[:, dffc * 512:(dffc + 1) * 512], OP.add, OP.mult,
                         r=("pm%d" % (4 + dffc), bk[1], "scrG.a"),
                         w=("h%d" % hb, "pm%d" % (4 + dffc)))
                    if not shared:
                        vtt(tgt, tgt,
                            slotB_b[:, 4096 + hb * 2048 + tc * 512:
                                    4096 + hb * 2048 + (tc + 1) * 512],
                            OP.mult, r=("h%d" % hb, "gbc%d" % hb),
                            w=("h%d" % hb,))
                for sub in range(2):
                    for q in range(4):
                        pq = sub * 4 + q
                        pk = "pm%d" % pq
                        mms = []
                        if shared and ie == 0:
                            for c2 in range(2):
                                mms.append((pm[pq][:, c2 * 256:(c2 + 1) * 256],
                                            brow[0:1, 1024 + (q * 2 + c2) * 128:
                                                 1024 + (q * 2 + c2 + 1) * 128],
                                            ones_bf[0:1, 0:256], ("sb2r",)))
                        if shared and ie == 1:
                            for c2 in range(2):
                                mms.append((pm[pq][:, c2 * 256:(c2 + 1) * 256],
                                            rowsall_bf[64:80,
                                                       2048 + (q * 2 + c2) * 128:
                                                       2048 + (q * 2 + c2 + 1) * 128],
                                            rowsall_bf[64:80,
                                                       tc * 512 + sub * 256:
                                                       tc * 512 + sub * 256 + 256],
                                            ("b2sb", "gwfm" + a_)))
                        for c2 in range(2):
                            for dffc in range(4):
                                mms.append((pm[pq][:, c2 * 256:(c2 + 1) * 256],
                                            slotB[:, W2C0 + dffc * 1024 +
                                                  (q * 2 + c2) * 128:
                                                  W2C0 + dffc * 1024 +
                                                  (q * 2 + c2 + 1) * 128],
                                            slotB_b[:, HB + hb * 2048 +
                                                    dffc * 512 + sub * 256:
                                                    HB + hb * 2048 +
                                                    dffc * 512 + sub * 256 + 256],
                                            ("w2b%d" % hb, "h%d" % hb)))
                        for i2, (o2, l2, r2, rk2) in enumerate(mms):
                            mm(o2, l2, r2, start=(i2 == 0),
                               stop=(i2 == len(mms) - 1), r=rk2, w=(pk,))
                        xv = x.ap().rearrange("p (c t) -> p c t", t=2048)[
                            :, 2 * q:2 * q + 2,
                            tc * 512 + sub * 256: tc * 512 + sub * 256 + 256]
                        vstt(xv, pm[pq][:, 0:512].rearrange(
                            "p (c t) -> p c t", t=256), 0.5, xv,
                            OP.mult, OP.add, r=(pk, "x"), w=("x", pk))
        p.barrier()


    if STOP_AFTER >= 4:
        moe(1)

    def mamba():
        u = ln_stats(want_rows=True)
        ln_fm_xln(u)
        DMA("gpsimd", scrG_bf[:, 1024:2560].rearrange(
            "p (c r) -> p c r", r=96),
            dI["wxp"].ap().rearrange("(c p) r -> p c r", p=128),
            r=(), w=("wxp", "scrG.b"))
        DMA("gpsimd", slotB_b[0:64, 8192:10240], dI["wdt"].ap(),
            r=(), w=("wdtsb", "slotB.wdt"))
        for qt in range(4):
            for ck in range(8):
                DMA("gpsimd", wA[:, ck * 1024:(ck + 1) * 1024],
                    dI["win"].ap()[ck * 128:(ck + 1) * 128,
                                   qt * 1024:(qt + 1) * 1024],
                    r=(), w=("wi.%d" % ck,))
            for tc in range(4):
                for oc in range(8):
                    pq = oc % 4
                    pk = "pm%d" % pq
                    for ck in range(8):
                        mm(pm[pq][:, 0:512],
                           wA[:, ck * 1024 + oc * 128: ck * 1024 + (oc + 1) * 128],
                           slotA[:, ck * 2048 + tc * 512: ck * 2048 + (tc + 1) * 512],
                           start=(ck == 0), stop=(ck == 7),
                           r=("wi.%d" % ck, "xln_%d" % ck), w=(pk,))
                    sl_ = qtm[:, (oc % 2) * 512:(oc % 2 + 1) * 512]
                    vcopy(sl_, pm[pq][:, 0:512], r=(pk,),
                          w=("xzst%d" % (oc % 2), pk))
                    ch0 = qt * 1024 + oc * 128
                    if qt < 2:
                        DMA("gpsimd", st_xin.ap()[ch0:ch0 + 128,
                                                  tc * 512:(tc + 1) * 512],
                            sl_, r=("xzst%d" % (oc % 2),),
                            w=("stxin%d_%d" % (tc, (ch0 // 128)),
                               "xzst%d" % (oc % 2)))
                    else:
                        zc = ch0 - 2048
                        DMA("gpsimd", mz_in[tc].ap()[zc // 1024, 2,
                                                     (zc % 1024):
                                                     (zc % 1024) + 128, :],
                            sl_, r=("xzst%d" % (oc % 2),),
                            w=("mzz%d" % tc, "xzst%d" % (oc % 2)))
        DMA("sync", hal_in.ap().rearrange("c (j k) -> c j k", k=3),
            st_xin.ap().rearrange("c (j i) -> c j i", i=128)[:, :, 125:128],
            r=tuple("stxin%d_%d" % (tc, cc) for tc in range(4)
                    for cc in range(16)),
            w=("halin",))
        CC("AllGather", OP.bypass, [hal_in.ap()], [hal_out.ap()],
           r=("halin",), w=("halout",))
        for tc in range(4):
            for cc in range(16):
                pad = scrF_bf[:, 0:524].rearrange("p (j i) -> p j i", i=131)
                DMA("sync", pad[:, :, 3:131],
                    st_xin.ap()[cc * 128:(cc + 1) * 128,
                                tc * 512:(tc + 1) * 512]
                    .rearrange("p (j i) -> p j i", i=128),
                    r=("stxin%d_%d" % (tc, cc), "convdone"), w=("pad",))
                DMA("sync", scrF_bf[:, 524:536],
                    hal_out.ap()[0, cc * 128:(cc + 1) * 128,
                                 tc * 12:(tc + 1) * 12],
                    r=("halout",), w=("candA",))
                if tc == 0:
                    V_(lambda e: e.memset(scrF_bf[:, 536:539], 0.0),
                       r=(), w=("candB",))
                    DMA("sync", scrF_bf[:, 539:548],
                        hal_out.ap()[1, cc * 128:(cc + 1) * 128, 0:9],
                        r=("halout",), w=("candB",))
                else:
                    DMA("sync", scrF_bf[:, 536:548],
                        hal_out.ap()[1, cc * 128:(cc + 1) * 128,
                                     tc * 12 - 3:(tc + 1) * 12 - 3],
                        r=("halout",), w=("candB",))
                vts(scrF_bf[:, 548:560], scrF_bf[:, 524:536],
                    gcols[:, 0:1], OP.mult, None, None,
                    r=("candA", "gc"), w=("candt",))
                vstt(pad[:, :, 0:3],
                     scrF_bf[:, 536:548].rearrange("p (j i) -> p j i", i=3),
                     gcols[:, 1:2],
                     scrF_bf[:, 548:560].rearrange("p (j i) -> p j i", i=3),
                     OP.mult, OP.add, r=("candB", "gc", "candt", "pad"),
                     w=("pad",))
                cwc = lambda k: small[:, 128 + cc * 4 + k:129 + cc * 4 + k]
                vts(scrG[:, 0:512].rearrange("p (j i) -> p j i", i=128),
                    pad[:, :, 0:128], cwc(0), OP.mult, None, None,
                    r=("pad", "cwc"), w=("scrG.cv",))
                for k in range(1, 4):
                    vstt(scrG[:, 0:512].rearrange("p (j i) -> p j i", i=128),
                         pad[:, :, k:k + 128], cwc(k),
                         scrG[:, 0:512].rearrange("p (j i) -> p j i", i=128),
                         OP.mult, OP.add, r=("pad", "cwc", "scrG.cv"),
                         w=("scrG.cv",))
                act(slotB_b[:, cc * 512:(cc + 1) * 512], scrG[:, 0:512],
                    AF.Silu, bias=small[:, 192 + cc:193 + cc],
                    r=("scrG.cv", "cbc"), w=("xc%d" % cc, "slotB.xc"))
                DMA("gpsimd", mz_in[tc].ap()[cc // 8, 0,
                                             (cc % 8) * 128:(cc % 8) * 128 + 128,
                                             :],
                    slotB_b[:, cc * 512:(cc + 1) * 512],
                    r=("xc%d" % cc,), w=("mzx%d" % tc,))
            pk = "pm6"
            for ck in range(16):
                mm(pm[6][0:96, 0:512],
                   scrG_bf[:, 1024 + ck * 96:1024 + (ck + 1) * 96],
                   slotB_b[:, ck * 512:(ck + 1) * 512],
                   start=(ck == 0), stop=(ck == 15),
                   r=("wxp", "xc%d" % ck), w=(pk,))
            vcopy(qtm[0:64, 0:512], pm[6][0:64, 0:512], r=(pk,),
                  w=("dtr", "qtm.0", "xzst0", "xzst1"))
            vcopy(scrG_bf[64:96, 2560:3072], pm[6][64:96, 0:512],
                  r=(pk,), w=("bcst", pk, "scrG.bc"))
            DMA("gpsimd", bc_in.ap()[:, tc * 512:(tc + 1) * 512],
                scrG_bf[64:96, 2560:3072], r=("bcst",), w=("bcin%d" % tc,))
            for oc in range(16):
                pq = 7
                pk = "pm7"
                mm(pm[7][:, 0:512], slotB_b[0:64, 8192 + oc * 128:
                                            8192 + (oc + 1) * 128],
                   qtm[0:64, 0:512], start=True, stop=True,
                   r=("wdtsb", "dtr"), w=(pk,))
                sl_ = slotB_b[:, 10240 + (oc % 2) * 512:
                              10240 + (oc % 2 + 1) * 512]
                act(scrF[:, 1024:1536], pm[7][:, 0:512], AF.Exp,
                    bias=small[:, 208 + oc:209 + oc],
                    r=(pk, "bdtc"), w=("scrF.sp", pk))
                vts(scrF[:, 1024:1536], scrF[:, 1024:1536], 1.0, OP.add,
                    None, None, r=("scrF.sp",), w=("scrF.sp",))
                act(sl_, scrF[:, 1024:1536], AF.Ln,
                    r=("scrF.sp",), w=("dtst%d" % (oc % 2),))
                DMA("gpsimd", mz_in[tc].ap()[oc // 8, 1,
                                             (oc % 8) * 128:(oc % 8) * 128 + 128,
                                             :],
                    sl_, r=("dtst%d" % (oc % 2),),
                    w=("mzd%d" % tc, "dtst%d" % (oc % 2)))
            p.emit("vector", lambda e: e.nop(),
                   tuple("xc%d" % cc for cc in range(16)), ("convdone",), "op")
            CC("AllGather", OP.bypass, [mz_in[tc].ap()], [mz_out[tc].ap()],
               r=("mzx%d" % tc, "mzd%d" % tc, "mzz%d" % tc),
               w=("mzo%d" % tc,))
        CC("AllGather", OP.bypass, [bc_in.ap()], [bc_out.ap()],
           r=tuple("bcin%d" % tc for tc in range(4)), w=("bcout",))
        for g2 in range(2):
            DMA("sync", bc_rep.ap().rearrange("r (b s i) -> r b s i",
                                              s=2, i=128)[:, :, g2, :],
                bc_out.ap()[g2].rearrange("r (b i) -> r b i", i=128),
                r=("bcout",), w=("bcrep",))
        p.barrier()

        for ck in range(8):
            DMA("gpsimd", wA[:, ck * 1024:(ck + 1) * 1024],
                dI["wout_s"].ap()[ck * 128:(ck + 1) * 128, :],
                r=(), w=("wo.%d" % ck,))
        XC, DT, DX, AA, BB, HH, ZZ = 0, 4096, 8192, 12288, 13312, 14336, 15360
        V_(lambda e: e.memset(hcarry[:], 0.0), r=(),
           w=tuple("hc%d_%d" % (n, dc) for n in range(NSTATE)
                   for dc in range(8)))
        for gc in range(8):
            tci, sb2_ = gc // 2, (gc % 2) * 256
            CA = 12288

            def merge_ld(which, row0, nd, dstbase, kks, cbase=None,
                         extra_r=()):
                CAx = CA if cbase is None else cbase
                w_ = nd * 256
                for g2 in range(2):
                    for s2 in range(2):
                        DMA("sync",
                            slotB_b[:, CAx + (s2 * 2 + g2) * w_:
                                    CAx + (s2 * 2 + g2 + 1) * w_]
                            .rearrange("p (d i) -> p d i", i=256),
                            mz_out[tci].ap()[g2, s2, which,
                                             row0 * 128:(row0 + nd) * 128,
                                             sb2_:sb2_ + 256]
                            .rearrange("(d p) i -> p d i", p=128),
                            r=("mzo%d" % tci, "scandone", "cands", "sa0",
                               "sa1", "sb0", "sb1", "sh0", "sh1") + extra_r,
                            w=("cands",))
                for g2 in range(2):
                    ca = slotB_b[:, CAx + g2 * w_:CAx + (g2 + 1) * w_]\
                        .rearrange("p (d j i) -> p d j i", j=2, i=128)
                    cb = slotB_b[:, CAx + (2 + g2) * w_:CAx + (3 + g2) * w_]\
                        .rearrange("p (d j i) -> p d j i", j=2, i=128)
                    tmp = scrG[:, 0:w_].rearrange(
                        "p (d j i) -> p d j i", j=2, i=128)
                    vts(tmp, ca, gcols[:, 1:2], OP.mult, None, None,
                        r=("cands", "gc"), w=("scrG.sel",))
                    dst = slotB_b[:, dstbase:dstbase + nd * 512]\
                        .rearrange("p (d j s i) -> p d j s i", j=2, s=2,
                                   i=128)[:, :, :, g2, :]
                    vstt(dst, cb, gcols[:, 0:1], tmp, OP.mult, OP.add,
                         r=("cands", "gc", "scrG.sel"), w=kks + ("cands",))
            for dhalf in range(2):
                merge_ld(0, dhalf * 4, 4, XC + dhalf * 2048,
                         tuple("sxc%d" % dc for dc in
                               range(dhalf * 4, dhalf * 4 + 4)))
                merge_ld(1, dhalf * 4, 4, DT + dhalf * 2048,
                         tuple("sdt%d" % dc for dc in
                               range(dhalf * 4, dhalf * 4 + 4)))
            for dc in range(8):
                vtt(slotB_b[:, DX + dc * 512:DX + (dc + 1) * 512],
                    slotB_b[:, XC + dc * 512:XC + (dc + 1) * 512],
                    slotB_b[:, DT + dc * 512:DT + (dc + 1) * 512],
                    OP.mult, r=("sxc%d" % dc, "sdt%d" % dc),
                    w=("sdx%d" % dc,))
            for n in range(NSTATE):
                for wbc, wb0, kk in ((0, 0, "brep"), (1, 4096, "crep")):
                    DMA("scalar", scrF_bf[:, wb0 + (n % 4) * 1024:
                                          wb0 + (n % 4) * 1024 + 512],
                        bc_rep.ap()[wbc * 16 + n, gc * 512:(gc + 1) * 512]
                        .unsqueeze(0).broadcast_to([128, 512]),
                        r=("bcrep", "%s%d" % (kk, n % 4)),
                        w=("%s%d" % (kk, n % 4),))
                for dc in range(8):
                    ab = (n * 8 + dc) % 2
                    acol_ = small[:, 224 + dc * 16 + n:225 + dc * 16 + n]
                    act(slotB_b[:, AA + ab * 512:AA + (ab + 1) * 512],
                        slotB_b[:, DT + dc * 512:DT + (dc + 1) * 512],
                        AF.Exp, scale=acol_,
                        r=("sdt%d" % dc, "acol", "cands"),
                        w=("sa%d" % ab,))
                    vtt(slotB_b[:, BB + ab * 512:BB + (ab + 1) * 512],
                        slotB_b[:, DX + dc * 512:DX + (dc + 1) * 512],
                        scrF_bf[:, (n % 4) * 1024:(n % 4) * 1024 + 512],
                        OP.mult,
                        r=("sdx%d" % dc, "brep%d" % (n % 4), "cands"),
                        w=("sb%d" % ab,))
                    hslot = slotB_b[:, HH + ab * 512:HH + (ab + 1) * 512]
                    p.emit("vector",
                           lambda e, o=hslot,
                           a0=slotB_b[:, AA + ab * 512:AA + (ab + 1) * 512],
                           b0=slotB_b[:, BB + ab * 512:BB + (ab + 1) * 512],
                           i0=hcarry[:, n * 8 + dc:n * 8 + dc + 1]:
                           e.tensor_tensor_scan(out=o, data0=a0, data1=b0,
                                                initial=i0, op0=OP.mult,
                                                op1=OP.add),
                           ("sa%d" % ab, "sb%d" % ab, "hc%d_%d" % (n, dc),
                            "cands"),
                           ("sh%d" % ab,), "op")
                    vcopy(hcarry[:, n * 8 + dc:n * 8 + dc + 1],
                          hslot[:, 511:512], r=("sh%d" % ab,),
                          w=("hc%d_%d" % (n, dc),))
                    yv = slotA_f[:, dc * 512:(dc + 1) * 512]
                    if n == 0:
                        vtt(yv, hslot,
                            scrF_bf[:, 4096 + (n % 4) * 1024:
                                    4096 + (n % 4) * 1024 + 512],
                            OP.mult, r=("sh%d" % ab, "crep%d" % (n % 4)),
                            w=("ya%d" % dc, "slotA.xb"))
                    else:
                        vtt(slotB_b[:, BB + ab * 512:BB + (ab + 1) * 512],
                            hslot,
                            scrF_bf[:, 4096 + (n % 4) * 1024:
                                    4096 + (n % 4) * 1024 + 512],
                            OP.mult, r=("sh%d" % ab, "crep%d" % (n % 4)),
                            w=("sb%d" % ab,))
                        vtt(yv, yv,
                            slotB_b[:, BB + ab * 512:BB + (ab + 1) * 512],
                            OP.add, r=("ya%d" % dc, "sb%d" % ab),
                            w=("ya%d" % dc,))
            for dc in range(8):
                merge_ld(2, dc, 1, ZZ + (dc % 2) * 512,
                         ("sz%d" % (dc % 2),), cbase=HH,
                         extra_r=tuple("ya%d" % d2 for d2 in range(8)) +
                         ("sh0", "sh1"))
                zs = slotB_b[:, AA + 0 * 512:AA + 512]
                act(zs, slotB_b[:, ZZ + (dc % 2) * 512:ZZ + (dc % 2 + 1) * 512],
                    AF.Silu, r=("sz%d" % (dc % 2), "sa0"), w=("sa0",))
                yv = slotA_f[:, dc * 512:(dc + 1) * 512]
                vstt(yv, slotB_b[:, XC + dc * 512:XC + (dc + 1) * 512],
                     small[:, 480 + dc:481 + dc], yv, OP.mult, OP.add,
                     r=("sxc%d" % dc, "dpc", "ya%d" % dc),
                     w=("ya%d" % dc,))
                vtt(slotB_b[:, DX + dc * 512:DX + (dc + 1) * 512], yv, zs,
                    OP.mult, r=("ya%d" % dc, "sa0", "sdx%d" % dc),
                    w=("sy3%d" % dc,))
            for dmc in range(8):
                pq = dmc % 4
                pk = "pm%d" % pq
                for ck in range(8):
                    mm(pm[pq][:, 0:512],
                       wA[:, ck * 1024 + dmc * 128: ck * 1024 + (dmc + 1) * 128],
                       slotB_b[:, DX + ck * 512:DX + (ck + 1) * 512],
                       start=(ck == 0), stop=(ck == 7),
                       r=("wo.%d" % ck, "sy3%d" % ck), w=(pk,))
                sl_ = qtm[:, (dmc % 2) * 512:(dmc % 2 + 1) * 512]
                vcopy(sl_, pm[pq][:, 0:512], r=(pk,),
                      w=("wost%d" % (dmc % 2), pk, "qtm.0"))
                for g2 in range(2):
                    DMA("gpsimd",
                        wo_in[tci].ap()[g2, dmc * 128:(dmc + 1) * 128,
                                        sb2_:sb2_ + 256]
                        .rearrange("p (j i) -> p j i", i=128),
                        sl_.rearrange("p (j s i) -> p j s i", s=2, i=128)
                        [:, :, g2, :],
                        r=("wost%d" % (dmc % 2),),
                        w=("woin%d" % tci, "wost%d" % (dmc % 2)))
            p.emit("vector", lambda e: e.nop(),
                   tuple("sy3%d" % dc for dc in range(8)) +
                   tuple("sxc%d" % dc for dc in range(8)) +
                   ("sa0", "sa1", "sb0", "sb1", "sh0", "sh1", "sz0", "sz1",
                    "cands"),
                   ("scandone",), "op")
            if gc % 2 == 1:
                CC("AllGather", OP.bypass, [wo_in[tci].ap()],
                   [wo_out[tci].ap()], r=("woin%d" % tci,),
                   w=("woout%d" % tci,))
        for tci in range(4):
            for g2 in range(2):
                for s2 in range(2):
                    DMA("sync", slotB_b[:, (s2 * 2 + g2) * 4096:
                                        (s2 * 2 + g2 + 1) * 4096]
                        .rearrange("p (c t) -> p c t", t=512),
                        wo_out[tci].ap()[g2, s2].rearrange(
                            "(c p) t -> p c t", p=128),
                        r=("woout%d" % tci, "wfin", "scandone",
                           "wost0", "wost1"), w=("wfb",))
            for s2 in range(2):
                vtt(slotB_b[:, s2 * 4096:s2 * 4096 + 4096],
                    slotB_b[:, s2 * 4096:s2 * 4096 + 4096],
                    slotB_b[:, (s2 + 2) * 4096:(s2 + 3) * 4096],
                    OP.add, r=("wfb",), w=("wfb",))
            gsl = gcols[:, 1:2]
            vts(slotB_b[:, 0:4096], slotB_b[:, 0:4096], gsl, OP.mult,
                None, None, r=("wfb", "gc"), w=("wfb",))
            vstt(slotB_b[:, 0:4096], slotB_b[:, 4096:8192], gcols[:, 0:1],
                 slotB_b[:, 0:4096], OP.mult, OP.add,
                 r=("wfb", "gc"), w=("wfb",))
            xv = x.ap().rearrange("p (c t) -> p c t", t=2048)[
                :, :, tci * 512:(tci + 1) * 512]
            vtt(xv, xv, slotB_b[:, 0:4096].rearrange(
                "p (c t) -> p c t", t=512), OP.add,
                r=("wfb", "x"), w=("x", "wfin"))
        p.barrier()


    if STOP_AFTER >= 5:
        mamba()

    if STOP_AFTER >= 6:
        moe(2)


    if DBG is None:
        DMA("sync", out_fm.ap().rearrange("(c p) t -> p c t", p=128),
            x[:].rearrange("p (c t) -> p c t", t=T),
            r=("x",), w=("outw",))
    p.emit("sync", lambda e: e.nop(), ("outw",), (), "op", "final")

    p.prepare()
    with nc.Block() as block:
        @block.tensor
        def _(eng):
            p.emit_stream("tensor", eng, sems)

        @block.vector
        def _(eng):
            p.emit_stream("vector", eng, sems)

        @block.scalar
        def _(eng):
            p.emit_stream("scalar", eng, sems)

        @block.gpsimd
        def _(eng):
            p.emit_stream("gpsimd", eng, sems)

        @block.sync
        def _(eng):
            p.emit_stream("sync", eng, sems)
    es.close()
    return nc


_NC_CACHE = {}


def kernel(**inputs):
    per_core = _shard_host(inputs)
    if "nc" not in _NC_CACHE:
        _NC_CACHE["nc"] = build_nc()
    nc = _NC_CACHE["nc"]
    res = run_bass_kernel_spmd(nc, per_core, core_ids=list(range(8)))
    return _gather_host(res.results)


# version 05 (final)
# speedup vs baseline: 1.3694x; 1.3694x over previous
import numpy as np
import ml_dtypes
import concourse.bass as bass
import concourse.mybir as mybir
from concourse.alu_op_type import AluOpType as OP
from concourse.bass_utils import run_bass_kernel_spmd
from concourse.aot_env import allow_ici_all_to_all
from contextlib import ExitStack

F32 = mybir.dt.float32
BF16 = mybir.dt.bfloat16
I32 = mybir.dt.int32
AF = mybir.ActivationFunctionType
AX = mybir.AxisListType

D = 1024
T = 2048
S = 4096
NB = 16
H = 16
HD = 64
DI = 2048
DIH = 1024
NSTATE = 16
DTR = 64
E = 16
DFF = 512

STOP_AFTER = 99
ATT_SUB = 99
ATT_HEADS = 16
SKIP_P1 = False
DBG = None

ENGS = ("tensor", "vector", "scalar", "gpsimd", "sync")
DMAQ = {"sync": "dq_sync", "scalar": "dq_scal", "gpsimd": "dq_gp"}


class Prog:

    def __init__(self):
        self.rec = []
        self.keys = {}
        self.eng_pos = {e: [] for e in ENGS}

    def emit(self, eng, fn, reads=(), writes=(), kind="op", tag=""):
        idx = len(self.rec)
        deps = set()
        for k in reads:
            st = self.keys.get(k)
            if st and st["w"] is not None:
                deps.add(st["w"])
        for k in writes:
            st = self.keys.get(k)
            if st:
                if st["w"] is not None:
                    deps.add(st["w"])
                deps.update(st["r"])
        deps.discard(idx)
        r = dict(eng=eng, fn=fn, kind=kind, deps=deps, idx=idx,
                 flagged=(kind != "op"), tag=tag, reads=tuple(reads),
                 writes=tuple(writes))
        self.rec.append(r)
        self.eng_pos[eng].append(idx)
        for k in reads:
            self.keys.setdefault(k, {"w": None, "r": []})["r"].append(idx)
        for k in writes:
            self.keys[k] = {"w": idx, "r": []}
        return idx

    def prepare(self):
        for r in self.rec:
            for d in r["deps"]:
                dr = self.rec[d]
                if dr["eng"] != r["eng"] and dr["kind"] == "op":
                    dr["flagged"] = True
        self.ccval = {}
        cccnt = 0
        for r in self.rec:
            if r["kind"] == "cc":
                cccnt += 1
                self.ccval[r["idx"]] = cccnt
        self.opval = {}
        self.op_prewait = {}
        for q in ENGS:
            ops = [r["idx"] for r in self.rec
                   if r["kind"] == "op" and r["eng"] == q and r["flagged"]]
            if not ops:
                continue
            oset = set(ops)
            groups = []
            cur = []
            need_close = False
            for r in self.rec:
                if r["idx"] in oset:
                    if need_close and cur:
                        groups.append(cur)
                        cur = []
                        need_close = False
                    cur.append(r["idx"])
                else:
                    if any(d in cur for d in r["deps"]):
                        need_close = True
            if cur:
                groups.append(cur)
            tot = 0
            for grp in groups:
                if tot > 0:
                    self.op_prewait[grp[0]] = ("pg_" + q, tot)
                tot += len(grp)
                for d in grp:
                    self.opval[d] = tot
        NSEM = 4
        self.dma_sem = {}
        self.dma_wval = {}
        self.dma_prewait = {}
        for q in ("sync", "scalar", "gpsimd"):
            dmas = [r["idx"] for r in self.rec
                    if r["kind"] == "dma" and r["eng"] == q]
            if not dmas:
                continue
            dset = set(dmas)
            closers = set()
            for r in self.rec:
                if r["kind"] == "dma":
                    continue
                for d in r["deps"]:
                    if d in dset:
                        closers.add(d)
            groups = []
            cur = []
            closed_at = -1
            for r in self.rec:
                if r["idx"] in dset:
                    cur.append(r["idx"])
                elif cur and any(d in closers and d in cur for d in r.get("deps", ())):
                    groups.append(cur)
                    cur = []
            groups = []
            cur = []
            need_close = False
            for r in self.rec:
                if r["idx"] in dset:
                    if (need_close or any(d in cur for d in r["deps"])) and cur:
                        groups.append(cur)
                        cur = []
                        need_close = False
                    cur.append(r["idx"])
                else:
                    if any(d in cur for d in r["deps"]):
                        need_close = True
            if cur:
                groups.append(cur)
            semtot = {}
            for gi, grp in enumerate(groups):
                snm = "dq_%s_%d" % (q, gi % NSEM)
                tot = semtot.get(snm, 0) + 16 * len(grp)
                prev = semtot.get(snm, 0)
                if prev > 0:
                    self.dma_prewait[grp[0]] = (snm, prev)
                semtot[snm] = tot
                for d in grp:
                    self.dma_sem[d] = snm
                    self.dma_wval[d] = tot

    def emit_stream(self, e, eng, sems):
        wm = {}
        last_drain = -1

        def sem_of(dep):
            dr = self.rec[dep]
            if dr["kind"] == "dma":
                return self.dma_sem[dep], self.dma_wval[dep]
            if dr["kind"] == "cc":
                return "cc", self.ccval[dep]
            return "pg_" + dr["eng"], self.opval[dep]

        for ri in self.eng_pos[e]:
            r = self.rec[ri]
            need_drain = False
            waits = {}
            for d in sorted(r["deps"]):
                dr = self.rec[d]
                if dr["eng"] == e and dr["kind"] == "op":
                    if d > last_drain and e != "tensor":
                        need_drain = True
                else:
                    s_, v = sem_of(d)
                    if wm.get(s_, 0) < v:
                        waits[s_] = max(waits.get(s_, 0), v)
            if r["idx"] in self.op_prewait:
                s2, v2 = self.op_prewait[r["idx"]]
                if wm.get(s2, 0) < v2:
                    waits[s2] = max(waits.get(s2, 0), v2)
            if need_drain:
                eng.drain()
                last_drain = r["idx"] - 1
            for s_, v in waits.items():
                eng.wait_ge(sems[s_], v)
                wm[s_] = v
            if r["kind"] == "dma" and r["idx"] in self.dma_prewait:
                s2, v2 = self.dma_prewait[r["idx"]]
                if wm.get(s2, 0) < v2:
                    eng.wait_ge(sems[s2], v2)
                    wm[s2] = v2
            inst = r["fn"](eng)
            if r["kind"] == "dma":
                inst.then_inc(sems[self.dma_sem[r["idx"]]], 16)
            elif r["kind"] == "cc":
                inst.then_inc(sems["cc"], 1)
            elif r["flagged"]:
                inst.then_inc(sems["pg_" + e], 1)

    def barrier(self):
        ph = len(self.rec)
        for e in ENGS:
            self.emit(e, lambda eng: eng.nop(), reads=(),
                      writes=("_bar_%d_%s" % (ph, e),), tag="bar")
        for e in ENGS:
            self.emit(e, lambda eng: eng.nop(),
                      reads=tuple("_bar_%d_%s" % (ph, x) for x in ENGS),
                      writes=(), tag="bar2")


def _shard_host(inputs):
    x = np.ascontiguousarray(inputs["L0_attn_x"])
    per_core = []
    cc = np.ascontiguousarray

    shared = {}
    shared["wq_t"] = cc(inputs["L0_attn_Wq"].T)
    shared["wk_t"] = cc(inputs["L0_attn_Wk"].T)
    shared["wv_t"] = cc(inputs["L0_attn_Wv"].T)
    shared["wo_t"] = cc(inputs["L0_attn_Wo"].T)
    shared["l0w"] = cc(inputs["L0_ffn_weight"])
    shared["l0b"] = cc(inputs["L0_ffn_bias"])
    shared["gate_wt"] = cc(inputs["L1_ffn_gate_w"].T)
    shared["ebias"] = cc(inputs["L1_ffn_expert_bias"])
    w1t = inputs["L1_ffn_w1"].transpose(0, 2, 1)
    w3t = inputs["L1_ffn_w3"].transpose(0, 2, 1)
    w2t = inputs["L1_ffn_w2"].transpose(0, 2, 1)
    shared["w1t"] = cc(w1t)
    shared["w3t"] = cc(w3t)
    shared["w2t"] = cc(w2t)
    shared["b1"] = cc(inputs["L1_ffn_b1"])
    shared["b3"] = cc(inputs["L1_ffn_b3"])
    shared["b2"] = cc(inputs["L1_ffn_b2"])
    shared["sw1t"] = cc(inputs["L1_ffn_sw1"].T)
    shared["sw3t"] = cc(inputs["L1_ffn_sw3"].T)
    shared["sw2t"] = cc(inputs["L1_ffn_sw2"].T)
    shared["sb1"] = cc(inputs["L1_ffn_sb1"])
    shared["sb3"] = cc(inputs["L1_ffn_sb3"])
    shared["sb2"] = cc(inputs["L1_ffn_sb2"])
    shared["win"] = cc(inputs["L1_attn_W_in"])
    shared["convw"] = cc(inputs["L1_attn_conv_w"])
    shared["convb"] = cc(inputs["L1_attn_conv_b"])
    shared["wxp"] = cc(inputs["L1_attn_W_xproj"])
    shared["wdt"] = cc(inputs["L1_attn_W_dt"])
    shared["bdt"] = cc(inputs["L1_attn_b_dt"])
    shared["b1c"] = cc(inputs["L1_ffn_b1"].reshape(E, 4, 128)
                       .transpose(2, 0, 1).reshape(128, 64))
    shared["b3c"] = cc(inputs["L1_ffn_b3"].reshape(E, 4, 128)
                       .transpose(2, 0, 1).reshape(128, 64))
    shared["cwc"] = cc(inputs["L1_attn_conv_w"].reshape(16, 128, 4)
                       .transpose(1, 0, 2).reshape(128, 64))
    shared["cbc"] = cc(inputs["L1_attn_conv_b"].reshape(16, 128).T)
    shared["bdtc"] = cc(inputs["L1_attn_b_dt"].reshape(16, 128).T)
    shared["sb1c"] = cc(inputs["L1_ffn_sb1"].reshape(8, 128).T)
    shared["sb3c"] = cc(inputs["L1_ffn_sb3"].reshape(8, 128).T)

    alog = inputs["L1_attn_A_log"]
    dp = inputs["L1_attn_Dp"]
    wout = inputs["L1_attn_W_out"]

    for c in range(8):
        seq, g = c >> 1, c & 1
        xs = x[seq].reshape(32, 128, D)[g::2]
        x_fm = cc(xs.reshape(T, D).T)
        d = dict(shared)
        d["x_fm"] = x_fm
        d["gvec"] = np.array([g, 1.0 - g, g * 128.0, 0.0], dtype=np.float32)
        lo, hi = g * DIH, (g + 1) * DIH
        d["alog_s"] = cc(alog[lo:hi])
        d["alc"] = cc(alog[lo:hi].reshape(8, 128, NSTATE)
                      .transpose(1, 0, 2).reshape(128, 8 * NSTATE))
        d["dpc"] = cc(dp[lo:hi].reshape(8, 128).T)
        d["dp_s"] = cc(dp[lo:hi])
        d["wout_s"] = cc(wout[lo:hi])
        per_core.append(d)
    return per_core


def _gather_host(results):
    out = np.empty((4, S, D), dtype=np.float32)
    for c in range(8):
        seq, g = c >> 1, c & 1
        o = np.asarray(results[c]["out_fm"])
        ot = o.T.reshape(NB, 128, D)
        out[seq].reshape(32, 128, D)[g::2] = ot
    return out


def build_nc():
    nc = bass.Bass("TRN2", target_bir_lowering=False, debug=False)
    p = Prog()

    dI = {}
    def din(name, shape):
        dI[name] = nc.dram_tensor(name, list(shape), F32, kind="ExternalInput")
        return dI[name]
    for nm, shp in [
        ("x_fm", (D, T)), ("gvec", (4,)),
        ("wq_t", (D, D)), ("wk_t", (D, D)), ("wv_t", (D, D)), ("wo_t", (D, D)),
        ("l0w", (D, D)), ("l0b", (D,)),
        ("gate_wt", (D, E)), ("ebias", (E,)),
        ("w1t", (E, D, DFF)), ("w3t", (E, D, DFF)), ("w2t", (E, DFF, D)),
        ("b1", (E, DFF)), ("b3", (E, DFF)), ("b2", (E, D)),
        ("sw1t", (D, D)), ("sw3t", (D, D)), ("sw2t", (D, D)),
        ("sb1", (D,)), ("sb3", (D,)), ("sb2", (D,)),
        ("win", (D, 2 * DI)), ("convw", (DI, 4)), ("convb", (DI,)),
        ("wxp", (DI, 96)), ("wdt", (DTR, DI)), ("bdt", (DI,)),
        ("alog_s", (DIH, NSTATE)), ("dp_s", (DIH,)), ("wout_s", (DIH, D)),
        ("b1c", (128, 64)), ("b3c", (128, 64)), ("cwc", (128, 64)),
        ("cbc", (128, 16)), ("bdtc", (128, 16)), ("sb1c", (128, 8)),
        ("sb3c", (128, 8)), ("alc", (128, 8 * NSTATE)), ("dpc", (128, 8)),
    ]:
        din(nm, shp)
    out_fm = nc.dram_tensor("out_fm", [D, T], F32, kind="ExternalOutput")

    kv_agin = nc.dram_tensor("kv_agin", [2, 128, 16384], BF16)
    kv_agout = nc.dram_tensor("kv_agout", [2, 2, 128, 16384], BF16)
    st_xin = nc.dram_tensor("st_xin", [DI, T], BF16)
    hal_in = nc.dram_tensor("hal_in", [DI, 48], BF16)
    hal_out = nc.dram_tensor("hal_out", [2, DI, 48], BF16)
    bc_in = nc.dram_tensor("bc_in", [32, T], BF16)
    bc_out = nc.dram_tensor("bc_out", [2, 32, T], BF16)
    rope_dr = nc.dram_tensor("rope_dr", [2, 2, 32, 2, 2048], F32)
    ln_dr = nc.dram_tensor("ln_dr", [2, 2048], F32)
    gw_dr = nc.dram_tensor("gw_dr", [16, 2048], BF16)
    bc_rep = nc.dram_tensor("bc_rep", [32, 4096], BF16)
    d_dr = nc.dram_tensor("d_dr", [16, 2048], F32)
    mz_in = [nc.dram_tensor("mz_in%d" % i, [2, 3, DIH, 512], BF16) for i in range(4)]
    mz_out = [nc.dram_tensor("mz_out%d" % i, [2, 2, 3, DIH, 512], BF16) for i in range(4)]
    wo_in = [nc.dram_tensor("wo_in%d" % i, [2, D, 512], BF16) for i in range(4)]
    wo_out = [nc.dram_tensor("wo_out%d" % i, [2, 2, D, 512], BF16) for i in range(4)]

    es = ExitStack()
    sb = lambda n, shp, dt: es.enter_context(nc.sbuf_tensor(n, list(shp), dt))
    x = sb("x", (128, 16384), F32)
    slotA = sb("slotA", (128, 16384), BF16)
    slotB = sb("slotB", (128, 16384), BF16)
    wA = sb("wA", (128, 8192), BF16)
    wdT = sb("wdT", (128, 8192), BF16)
    scrF = sb("scrF", (128, 4096), F32)
    scrG = sb("scrG", (128, 2048), F32)
    rowsall = sb("rowsall", (128, 2048), F32)
    brow = sb("brow", (128, 2048), BF16)
    ones_bf = sb("ones_bf", (1, 512), BF16)
    onesc_bf = sb("onesc_bf", (128, 1), BF16)
    ident = sb("ident", (128, 128), BF16)
    gcols = sb("gcols", (128, 4), F32)
    masks = sb("masks", (128, 256), BF16)
    small = sb("small", (128, 1024), F32)
    hcarry = sb("hcarry", (128, 256), F32)
    pm = [es.enter_context(nc.psum_tensor("pm%d" % i, [128, 512], F32))
          for i in range(8)]

    sems = {}
    for nm in (["pg_tensor", "pg_vector", "pg_scalar", "pg_gpsimd", "pg_sync",
                "cc"]
               + ["dq_%s_%d" % (q, i) for q in ("sync", "scalar", "gpsimd")
                  for i in range(4)]):
        sems[nm] = es.enter_context(nc.semaphore(nm))

    def T_(fn, r=(), w=(), tag=""): p.emit("tensor", fn, r, w, "op", tag)
    def V_(fn, r=(), w=(), tag=""): p.emit("vector", fn, r, w, "op", tag)
    def SC(fn, r=(), w=(), tag=""): p.emit("scalar", fn, r, w, "op", tag)
    def G_(fn, r=(), w=(), tag=""): p.emit("gpsimd", fn, r, w, "op", tag)
    def DMA(eng, out_ap, in_ap, r=(), w=(), tag="", slow=False):
        p.emit(eng, lambda e, o=out_ap, i=in_ap, s_=slow: e.dma_start(
            out=o, in_=i, **({"allow_slow_non_contiguous": True} if s_ else {})),
               r, w, "dma", tag)
    def CC(kind, op, ins, outs, r=(), w=(), tag="cc"):
        def f(e, k=kind, o=op, i=ins, u=outs):
            with allow_ici_all_to_all():
                return e.collective_compute(
                    k, o, replica_groups=[[0, 1], [2, 3], [4, 5], [6, 7]],
                    ins=[a.opt() for a in i], outs=[a.opt() for a in u])
        p.emit("gpsimd", f, r, w, "cc", tag)

    def vcopy(out, in_, r, w, eng="vector"):
        p.emit(eng, lambda e, o=out, i=in_: e.tensor_copy(o, i), r, w, "op")
    def vts(out, in0, s1, op0, s2=None, op1=None, r=(), w=(), eng="vector"):
        def f(e, o=out, i=in0, a=s1, b=s2, o0=op0, o1=op1):
            return e.tensor_scalar(out=o, in0=i, scalar1=a, scalar2=b,
                                   op0=o0, **({"op1": o1} if o1 is not None else {}))
        p.emit(eng, f, r, w, "op")
    def vtt(out, in0, in1, op, r=(), w=(), eng="vector"):
        p.emit(eng, lambda e, o=out, a=in0, b=in1, q=op: e.tensor_tensor(
            out=o, in0=a, in1=b, op=q), r, w, "op")
    def vstt(out, in0, s, in1, op0, op1, r=(), w=(), eng="vector"):
        p.emit(eng, lambda e, o=out, a=in0, c=s, b=in1, q0=op0, q1=op1:
               e.scalar_tensor_tensor(out=o, in0=a, scalar=c, in1=b,
                                      op0=q0, op1=q1), r, w, "op")
    def act(out, in_, func, bias=None, scale=None, r=(), w=()):
        def f(e, o=out, i=in_, fu=func, b=bias, s=scale):
            kw = {}
            if b is not None:
                kw["bias"] = b
            if s is not None:
                kw["scale"] = s
            return e.activation(out=o, in_=i, func=fu, **kw)
        p.emit("scalar", f, r, w, "op")
    def mm(out, lhsT, rhs, start, stop, r=(), w=()):
        p.emit("tensor", lambda e, o=out, l=lhsT, m=rhs, s1=start, s2=stop:
               e.matmul(o, l, m, start=s1, stop=s2), r, w, "op")
    def tpose(out, in_, idn, r=(), w=()):
        p.emit("tensor", lambda e, o=out, i=in_, d=idn: e.transpose(o, i, d),
               r, w, "op")
    def red(out, in_, axis, op, absval=None, r=(), w=(), eng="vector"):
        def f(e, o=out, i=in_, a=axis, q=op, v=absval):
            kw = {"apply_absolute_value": v} if v is not None else {}
            return e.tensor_reduce(out=o, in_=i, axis=a, op=q, **kw)
        p.emit(eng, f, r, w, "op")

    identf = sb("identf", (128, 128), F32)
    qtm = sb("qtm", (128, 1024), BF16)
    scrG_i = None
    _sc = [528]
    def smalloc(n):
        a = _sc[0]; _sc[0] += n
        assert _sc[0] <= 1024
        return a
    C_QMX = smalloc(32); C_QS = smalloc(32)
    C_MS = smalloc(16); C_SQ = smalloc(16)
    C_MC = smalloc(16); C_SC = smalloc(16)
    C_T1 = smalloc(16); C_T2 = smalloc(16); C_T3 = smalloc(16)
    C_VAR = smalloc(16); C_INV = smalloc(16)
    C_SGN = smalloc(1)
    C_R75 = smalloc(1); C_QIV = smalloc(32)

    slotA_f = slotA.ap().bitcast(F32)
    slotB_f = slotB.ap().bitcast(F32)
    scrF_bf = scrF.ap().bitcast(BF16)
    scrG_bf = scrG.ap().bitcast(BF16)
    scrG_i = scrG.ap().bitcast(I32)
    slotB_b = slotB.ap()

    SWAPM = [i + 1 if i % 2 == 0 else i - 1 for i in range(32)]
    TP2 = 6.28125
    TP2B = float(np.float32(2 * np.pi - TP2))
    TP2B_s = float(np.float32(np.trunc((2 * np.pi - TP2) * 2 ** 20) / 2 ** 20))
    TP2C = float(np.float32(2 * np.pi - TP2 - TP2B_s))
    INV2PI = float(np.float32(1.0 / (2 * np.pi)))
    LH = float(np.float32(np.log(10000.0) / 32))
    LH1 = float(np.float32(np.trunc(LH * 2 ** 12) / 2 ** 12))
    LH2 = float(np.float32(np.log(10000.0) / 32 - LH1))
    PIH1 = float(np.float32(np.pi / 2))
    PIH2 = float(np.float32(np.pi / 2 - PIH1))

    DMA("sync", x[:].rearrange("p (c t) -> p c t", t=T),
        dI["x_fm"].ap().rearrange("(c p) t -> p c t", p=128),
        r=(), w=("x0", "x1", "x2", "x3", "x4", "x5", "x6", "x7"))
    DMA("sync", gcols[:, 0:4],
        dI["gvec"].ap().unsqueeze(0).broadcast_to([128, 4]), r=(), w=("gc",))
    for t_, val in ((ones_bf, 1.0), (onesc_bf, 1.0)):
        V_(lambda e, a=t_, v=val: e.memset(a[:], v), r=(), w=("ones",))
    G_(lambda e: e.iota(scrG[:, 0:128], pattern=[[-1, 128]], base=0,
                        channel_multiplier=1,
                        allow_small_or_imprecise_dtypes=True),
       r=(), w=("io1",))
    vts(ident[:], scrG[:, 0:128], 0.0, OP.is_equal, r=("io1",), w=("ident",))
    vts(identf[:], scrG[:, 0:128], 0.0, OP.is_equal, r=("io1",), w=("identf",))
    G_(lambda e: e.iota(scrG[:, 128:256], pattern=[[1, 128]], base=0,
                        channel_multiplier=-1,
                        allow_small_or_imprecise_dtypes=True),
       r=(), w=("io2",))
    vts(scrG[:, 256:384], scrG[:, 128:256], 0.0, OP.is_ge, r=("io2",), w=("tri",))
    vts(masks[:, 0:128], scrG[:, 256:384], gcols[:, 0:1], OP.max,
        r=("tri", "gc"), w=("mask0",))
    vts(masks[:, 128:256], scrG[:, 256:384], gcols[:, 0:1], OP.mult,
        r=("tri", "gc"), w=("mask1",))
    V_(lambda e: e.memset(small[0:1, 1023:1024], 1.0), r=(), w=("onef",))
    V_(lambda e: e.memset(small[:, C_R75:C_R75 + 1], 7.5), r=(), w=("c75",))
    p.emit("vector", lambda e: e.reciprocal(out=small[:, C_R75:C_R75 + 1],
                                            in_=small[:, C_R75:C_R75 + 1]),
           ("c75",), ("r75",), "op")
    V_(lambda e: e.memset(small[32:33, 1023:1024], 1.0), r=(), w=("onef",))
    DMA("sync", small[:, 0:64], dI["b1c"].ap(), r=(), w=("b1c",))
    DMA("sync", small[:, 64:128], dI["b3c"].ap(), r=(), w=("b3c",))
    DMA("sync", small[:, 128:192], dI["cwc"].ap(), r=(), w=("cwc",))
    DMA("sync", small[:, 192:208], dI["cbc"].ap(), r=(), w=("cbc",))
    DMA("sync", small[:, 208:224], dI["bdtc"].ap(), r=(), w=("bdtc",))
    DMA("sync", small[:, 224:352], dI["alc"].ap(), r=(), w=("alraw",))
    DMA("sync", small[:, 480:488], dI["dpc"].ap(), r=(), w=("dpc",))
    DMA("sync", small[0:16, 504:505], dI["ebias"].ap().unsqueeze(1),
        r=(), w=("ebc",))
    DMA("sync", small[:, 488:496], dI["sb1c"].ap(), r=(), w=("sb1c",))
    DMA("sync", small[:, 496:504], dI["sb3c"].ap(), r=(), w=("sb3c",))
    act(small[:, 224:352], small[:, 224:352], AF.Exp, r=("alraw",), w=("alx",))
    vts(small[:, 224:352], small[:, 224:352], -1.0, OP.mult, r=("alx",),
        w=("acol",))
    DMA("gpsimd", brow[0:1, 0:1024], dI["l0b"].ap().unsqueeze(0),
        r=(), w=("l0br",))
    DMA("gpsimd", brow[0:1, 1024:2048], dI["sb2"].ap().unsqueeze(0),
        r=(), w=("sb2r",))
    rowsall_bf = rowsall.ap().bitcast(BF16)

    p.barrier()
    DMA("sync", slotA_f[:, 0:8192].rearrange("p (c q) -> p c q", q=1024),
        dI["l0w"].ap().rearrange("(c p) q -> p c q", p=128),
        r=(), w=("wraw",))

    def quant_tile(src, dst_bf, rk, wk, sc0=528, sc1=560):
        mx = small[:, sc0:sc0 + 32]
        s_ = small[:, sc1:sc1 + 32]
        red(mx, src.rearrange("p (b k) -> p b k", k=32), AX.X, OP.max,
            absval=True, r=rk, w=("q_mx",))
        vts(s_, mx, 7.5, OP.divide, 1e-30, OP.max, r=("q_mx",), w=("q_s",))
        sb3 = s_.unsqueeze(2).broadcast_to([128, 32, 32])
        y = scrG[:, 1024:2048]
        vtt(y.rearrange("p (b k) -> p b k", k=32),
            src.rearrange("p (b k) -> p b k", k=32), sb3, OP.divide,
            r=rk + ("q_s",), w=("q_y",))
        a = scrG[:, 0:1024]
        act(a, y, AF.Abs, r=("q_y",), w=("q_a",))
        ge2, ge4, qi, qv = (scrF[:, 0:1024], scrF[:, 1024:2048],
                            scrF[:, 2048:3072], scrF[:, 3072:4096])
        vts(ge2, a, 2.0, OP.is_ge, r=("q_a",), w=("q_g2",))
        vts(ge4, a, 4.0, OP.is_ge, r=("q_a",), w=("q_g4",))
        vts(qi, ge2, -4.0, OP.mult, 8.0, OP.add, r=("q_g2",), w=("q_qi",))
        vstt(qi, ge4, -2.0, qi, OP.mult, OP.add, r=("q_g4", "q_qi"), w=("q_qi",))
        vts(qv, ge2, 0.125, OP.mult, 0.125, OP.add, r=("q_g2",), w=("q_qv",))
        vstt(qv, ge4, 0.25, qv, OP.mult, OP.add, r=("q_g4", "q_qv"), w=("q_qv",))
        z = scrF[:, 0:1024]
        vtt(z, y, qi, OP.mult, r=("q_y", "q_qi"), w=("q_z",))
        m_ = scrF[:, 1024:2048]
        vts(m_, z, 0.5, OP.add, 1.0, OP.mod, r=("q_z",), w=("q_m",))
        k_ = scrG[:, 0:1024]
        vstt(k_, z, 0.5, m_, OP.add, OP.subtract, r=("q_z", "q_m"), w=("q_k",))
        xdn = scrF[:, 0:1024]
        vtt(xdn, k_, qv, OP.mult, r=("q_k", "q_qv"), w=("q_xdn",))
        vtt(dst_bf.rearrange("p (b k) -> p b k", k=32),
            xdn.rearrange("p (b k) -> p b k", k=32), sb3, OP.mult,
            r=("q_xdn", "q_s"), w=wk)

    def prep_wquant():
        for ce in range(8):
            quant_tile(slotA_f[:, ce * 1024:(ce + 1) * 1024], ("wraw",),
                       slotB[:, ce * 1024:(ce + 1) * 1024], ("wdtm%d" % ce,))
        pmt = pm[0].ap().bitcast(BF16)
        for cd in range(8):
            for ce2 in range(8):
                k = (cd * 8 + ce2) % 2
                tpose(pmt[:, k * 128:(k + 1) * 128],
                      slotB[:, ce2 * 1024 + cd * 128:
                            ce2 * 1024 + (cd + 1) * 128], ident[:],
                      r=("wdtm%d" % ce2, "ident"), w=("pm0",))
                vcopy(wdT[:, cd * 1024 + ce2 * 128:
                          cd * 1024 + (ce2 + 1) * 128],
                      pmt[:, k * 128:(k + 1) * 128], r=("pm0",),
                      w=("wdT", "pm0"))
        p.barrier()

    ln_ct = [0]
    sl16 = lambda c0: small[:, c0:c0 + 16]

    def ln_stats(want_rows):
        ln_ct[0] += 1
        u = "L%d" % ln_ct[0]
        vcopy(slotA[:], x[:], r=("x0", "x1", "x2", "x3", "x4", "x5", "x6", "x7"), w=("slotA.xb",))
        vtt(slotB[:], x[:], x[:], OP.mult, r=("x0", "x1", "x2", "x3", "x4", "x5", "x6", "x7"), w=("slotB.xsq",))
        for which, src, kk in ((0, slotA, "slotA.xb"), (1, slotB, "slotB.xsq")):
            for q in range(4):
                pmi = (4 + q) if which == 0 else q
                for c in range(8):
                    mm(pm[pmi][0:1, 0:512], onesc_bf[:],
                       src[:, c * 2048 + q * 512: c * 2048 + (q + 1) * 512],
                       start=(c == 0), stop=(c == 7),
                       r=(kk, "ones"), w=("pm%d" % pmi,))
        for q in range(4):
            vts(rowsall[0:1, q * 512:(q + 1) * 512],
                pm[4 + q][0:1, 0:512], 1.0 / D, OP.mult,
                r=("pm%d" % (4 + q),), w=("mrow" + u, "pm%d" % (4 + q)))
            vts(rowsall[32:33, q * 512:(q + 1) * 512],
                pm[q][0:1, 0:512], 1.0 / D, OP.mult,
                r=("pm%d" % q,), w=("sqrow" + u, "pm%d" % q))
        for which2, rsrc, dst_c, kk2 in (
                (0, rowsall[0:1, :], C_MS, "mtm"), 
                (1, rowsall[32:33, :], C_SQ, "sqtm")):
            pmi2 = 4 + which2
            for t2 in range(16):
                tpose(pm[pmi2][:, t2:t2 + 1],
                      rsrc[:, t2 * 128:(t2 + 1) * 128],
                      small[32 * which2:32 * which2 + 1, 1023:1024],
                      r=(("mrow" if which2 == 0 else "sqrow") + u, "onef"),
                      w=("pm%d" % pmi2,))
            vcopy(small[:, dst_c:dst_c + 16], pm[pmi2][:, 0:16],
                  r=("pm%d" % pmi2,), w=(kk2 + u, "pm%d" % pmi2))
        vcopy(sl16(C_MC), sl16(C_MS), r=("mtm" + u,), w=("mc" + u,))
        vtt(sl16(C_T1), sl16(C_MS), sl16(C_MS), OP.mult, r=("mtm" + u,),
            w=("m2" + u,))
        vtt(sl16(C_VAR), sl16(C_SQ), sl16(C_T1), OP.subtract,
            r=("sqtm" + u, "m2" + u), w=("v0" + u,))
        vts(sl16(C_VAR), sl16(C_VAR), 1e-5, OP.add, r=("v0" + u,), w=("vE" + u,))
        p.emit("vector", lambda e: e.reciprocal(out=sl16(C_INV), in_=sl16(C_VAR)),
               ("vE" + u,), ("iv" + u,), "op")
        act(sl16(C_SC), sl16(C_INV), AF.Sqrt, r=("iv" + u,), w=("s0" + u,))
        vtt(sl16(C_T1), sl16(C_SC), sl16(C_SC), OP.mult, r=("s0" + u,),
            w=("nt1" + u,))
        vtt(sl16(C_T2), sl16(C_T1), sl16(C_VAR), OP.mult,
            r=("nt1" + u, "vE" + u), w=("nt2" + u,))
        vts(sl16(C_T3), sl16(C_T2), -0.5, OP.mult, 1.5, OP.add,
            r=("nt2" + u,), w=("nt3" + u,))
        vtt(sl16(C_SC), sl16(C_SC), sl16(C_T3), OP.mult,
            r=("s0" + u, "nt3" + u), w=("sc" + u,))
        if want_rows:
            for which2, src_c, kk2, kk3 in (
                    (0, C_MC, "mc", "mrowF"), (1, C_SC, "sc", "srowF")):
                for t2 in range(16):
                    q2 = t2 // 4
                    tpose(pm[q2][0:1, (t2 % 4) * 128:(t2 % 4 + 1) * 128],
                          small[:, src_c + t2:src_c + t2 + 1], identf[:],
                          r=(kk2 + u, "identf"), w=("pm%d" % q2,))
                for q2 in range(4):
                    vts(rowsall[32 * which2:32 * which2 + 1,
                                q2 * 512:(q2 + 1) * 512],
                        pm[q2][0:1, 0:512], 1.0, OP.mult, None, None,
                        r=("pm%d" % q2,), w=(kk3 + u, "pm%d" % q2))
        return u

    def ln_fm_xln(u):
        DMA("scalar", ln_dr.ap()[0:1, :], rowsall.ap()[0:1, :],
            r=("mrowF" + u,), w=("lndr0",))
        DMA("scalar", ln_dr.ap()[1:2, :], rowsall.ap()[32:33, :],
            r=("srowF" + u,), w=("lndr1",))
        DMA("scalar", scrF[:, 0:2048],
            ln_dr.ap()[0].unsqueeze(0).broadcast_to([128, 2048]),
            r=("lndr0",), w=("scrF.mb",))
        DMA("scalar", scrF[:, 2048:4096],
            ln_dr.ap()[1].unsqueeze(0).broadcast_to([128, 2048]),
            r=("lndr1",), w=("scrF.sb",))
        for c in range(8):
            for hh in range(2):
                sl_ = slice(c * 2048 + hh * 1024, c * 2048 + (hh + 1) * 1024)
                bsl = slice(hh * 1024, (hh + 1) * 1024)
                tb = scrG[:, 0:1024] if hh == 0 else scrG[:, 1024:2048]
                gk = "scrG.a" if hh == 0 else "scrG.b"
                vtt(tb, x[:, sl_], scrF[:, 0:2048][:, bsl], OP.subtract,
                    r=("x%d" % c, "scrF.mb"), w=(gk,))
                vtt(slotA[:, sl_], tb, scrF[:, 2048:4096][:, bsl], OP.mult,
                    r=(gk, "scrF.sb"),
                    w=("xln_%d" % c, "slotA.xb"))

    def quant_tile(src, src_keys, dst_bf, dst_keys):
        mx = small[:, C_QMX:C_QMX + 32]
        s_ = small[:, C_QS:C_QS + 32]
        red(mx, src.rearrange("p (b k) -> p b k", k=32), AX.X, OP.max,
            absval=True, r=src_keys, w=("q_mx",))
        vts(s_, mx, small[:, C_R75:C_R75 + 1], OP.mult, 1e-30, OP.max,
            r=("q_mx", "r75"), w=("q_s",))
        iv = small[:, C_QIV:C_QIV + 32]
        p.emit("vector", lambda e: e.reciprocal(out=iv, in_=s_),
               ("q_s",), ("q_iv",), "op")
        ivb = iv.unsqueeze(2).broadcast_to([128, 32, 32])
        sb3 = s_.unsqueeze(2).broadcast_to([128, 32, 32])
        y = scrG[:, 1024:2048]
        vtt(y.rearrange("p (b k) -> p b k", k=32),
            src.rearrange("p (b k) -> p b k", k=32), ivb, OP.mult,
            r=src_keys + ("q_iv",), w=("scrG.b",))
        a = scrG[:, 0:1024]
        act(a, y, AF.Abs, r=("scrG.b",), w=("scrG.a",) + src_keys)
        ge2, ge4, qi, qv = (scrF[:, 0:1024], scrF[:, 1024:2048],
                            scrF[:, 2048:3072], scrF[:, 3072:4096])
        vts(ge2, a, 2.0, OP.is_ge, None, None, r=("scrG.a",), w=("scrF.q0",))
        vts(ge4, a, 4.0, OP.is_ge, None, None, r=("scrG.a",), w=("scrF.q1",))
        vts(qi, ge2, -4.0, OP.mult, 8.0, OP.add, r=("scrF.q0",), w=("scrF.q2",))
        vstt(qi, ge4, -2.0, qi, OP.mult, OP.add,
             r=("scrF.q1", "scrF.q2"), w=("scrF.q2",))
        vts(qv, ge2, 0.125, OP.mult, 0.125, OP.add, r=("scrF.q0",),
            w=("scrF.q3",))
        vstt(qv, ge4, 0.25, qv, OP.mult, OP.add,
             r=("scrF.q1", "scrF.q3"), w=("scrF.q3",))
        z = scrG[:, 0:1024]
        vtt(z, y, qi, OP.mult, r=("scrG.b", "scrF.q2"), w=("scrG.a",))
        act(scrG_i[:, 1024:2048], z, AF.Identity, r=("scrG.a",),
            w=("scrG.b",))
        t1 = scrF[:, 0:1024]
        vtt(t1, scrG_i[:, 1024:2048], qv, OP.mult, r=("scrG.b", "scrF.q3"),
            w=("scrF.q0",))
        vtt(dst_bf.rearrange("p (b k) -> p b k", k=32),
            t1.rearrange("p (b k) -> p b k", k=32), sb3, OP.mult,
            r=("scrF.q0", "q_s"), w=dst_keys)

    def l0ffn(app):
        u = ln_stats(want_rows=False)
        for t_ in range(NB):
            for c in range(8):
                k = c % 2
                tpose(pm[k][:, 0:128],
                      x[:, c * 2048 + t_ * 128: c * 2048 + (t_ + 1) * 128],
                      identf[:],
                      r=("x%d" % c, "identf"), w=("pm%d" % k,))
                vts(scrG[:, c * 128:(c + 1) * 128], pm[k][:, 0:128],
                    small[:, C_MC + t_:C_MC + t_ + 1], OP.subtract,
                    small[:, C_SC + t_:C_SC + t_ + 1], OP.mult,
                    r=("pm%d" % k, "mc" + u, "sc" + u),
                    w=("scrG.a", "pm%d" % k))
            quant_tile(scrG[:, 0:1024], ("scrG.a",),
                       qtm[:, 0:1024], ("qtm",))
            pmtb = pm[2].ap().bitcast(BF16)
            for c in range(8):
                k = c % 2
                tpose(pmtb[:, k * 128:(k + 1) * 128],
                      qtm[:, c * 128:(c + 1) * 128], ident[:],
                      r=("qtm", "ident"), w=("pm2",))
                vcopy(slotB[:, c * 2048 + t_ * 128: c * 2048 + (t_ + 1) * 128],
                      pmtb[:, k * 128:(k + 1) * 128],
                      r=("pm2",), w=("xd%d_%d" % (c, t_), "slotB.xsq", "pm2"))
        for tc in range(4):
            for ce in range(8):
                pk = "pm%d" % (4 + ce % 2)
                pmm = pm[4 + ce % 2]
                for ck in range(8):
                    mm(pmm[:, 0:512],
                       wdT[:, ck * 1024 + ce * 128: ck * 1024 + (ce + 1) * 128],
                       slotB[:, ck * 2048 + tc * 512: ck * 2048 + (tc + 1) * 512],
                       start=(ck == 0), stop=False,
                       r=("wdT",) + tuple("xd%d_%d" % (ck, tt2)
                                          for tt2 in range(tc * 4, tc * 4 + 4)),
                       w=(pk,))
                mm(pmm[:, 0:512], brow[0:1, ce * 128:(ce + 1) * 128],
                   ones_bf[:], start=False, stop=True, r=("l0br",), w=(pk,))
                vstt(x[:, ce * 2048 + tc * 512: ce * 2048 + (tc + 1) * 512],
                     pmm[:, 0:512], 0.5,
                     x[:, ce * 2048 + tc * 512: ce * 2048 + (tc + 1) * 512],
                     OP.mult, OP.add, r=(pk, "x%d" % ce),
                     w=("x%d" % ce, pk))
        p.barrier()

    prep_wquant()

    def dbg_dump(src_ap, nelem, rkeys):
        o_ = (out_fm.ap().bitcast(BF16).rearrange("a b -> (a b)")
              .rearrange("(p f) -> p f", p=src_ap.shape[0]))
        DMA("sync", o_[:, 0:src_ap.shape[1]], src_ap, r=rkeys,
            w=("outw",) + ("x0", "x1", "x2", "x3", "x4", "x5", "x6", "x7"))

    if not SKIP_P1:
        l0ffn(1)

    def attention():
        G_(lambda e: e.iota(scrG[0:1, 0:32], pattern=[[1, 32]], base=0,
                            channel_multiplier=0,
                            allow_small_or_imprecise_dtypes=True),
           r=(), w=("scrG.a",))
        vts(scrG[0:1, 32:64], scrG[0:1, 0:32], -LH1, OP.mult, None, None,
            r=("scrG.a",), w=("scrG.a",))
        vstt(scrG[0:1, 64:96], scrG[0:1, 0:32], -LH2, scrG[0:1, 32:64],
             OP.mult, OP.add, r=("scrG.a",), w=("scrG.a",))
        act(scrG[0:1, 96:128], scrG[0:1, 64:96], AF.Exp,
            r=("scrG.a",), w=("invf",))
        G_(lambda e: e.iota(rowsall[0:1, 0:2048], pattern=[[256, 16], [1, 128]],
                            base=0, channel_multiplier=0,
                            allow_small_or_imprecise_dtypes=True),
           r=(), w=("tg0",))
        vts(rowsall[0:1, 0:2048], rowsall[0:1, 0:2048], gcols[0:1, 2:3],
            OP.add, None, None, r=("tg0", "gc"), w=("tgrow",))
        for q in range(4):
            mm(pm[q][0:32, 0:512], scrG[0:1, 96:128],
               rowsall[0:1, q * 512:(q + 1) * 512], start=True, stop=True,
               r=("invf", "tgrow"), w=("pm%d" % q,))
        for q in range(4):
            pk = "pm%d" % q
            th = pm[q][0:32, 0:512]
            r1 = scrG[0:32, 0:512]
            ki = scrG_i[0:32, 512:1024]
            ph = scrG[0:32, 1024:1536]
            wr = scrG[0:32, 1536:2048]
            for which, dst_base, shift in ((0, 64, 0.0), (1, 96, 0.25)):
                vts(r1, th, INV2PI, OP.mult, shift, OP.add,
                    r=(pk,), w=("scrG.a",))
                act(ki, r1, AF.Identity, r=("scrG.a",), w=("scrG.b",))
                vstt(ph, ki, -TP2, th, OP.mult, OP.add,
                     r=("scrG.b", pk), w=("scrG.b2",))
                vstt(ph, ki, -TP2B_s, ph, OP.mult, OP.add,
                     r=("scrG.b", "scrG.b2"), w=("scrG.b2",))
                vstt(ph, ki, -TP2C, ph, OP.mult, OP.add,
                     r=("scrG.b", "scrG.b2"), w=("scrG.b2",))
                if which == 1:
                    vts(ph, ph, PIH1, OP.add, None, None,
                        r=("scrG.b2",), w=("scrG.b2",))
                    vts(ph, ph, PIH2, OP.add, None, None,
                        r=("scrG.b2",), w=("scrG.b2",))
                vts(wr, ph, float(np.float32(np.pi)), OP.is_gt, None, None,
                    r=("scrG.b2",), w=("scrG.w",))
                vstt(ph, wr, -float(np.float32(2 * np.pi)), ph,
                     OP.mult, OP.add, r=("scrG.w", "scrG.b2"), w=("scrG.b2",))
                act(rowsall[dst_base:dst_base + 32, q * 512:(q + 1) * 512],
                    ph, AF.Sin, r=("scrG.b2",),
                    w=(("sintab",) if which == 0 else ("costab",), pk))
        u = ln_stats(want_rows=True)
        ln_fm_xln(u)
        for kk2 in range(2):
            for a2 in range(2):
                for b2 in range(2):
                    DMA("scalar", rope_dr.ap()[kk2, a2, :, b2, :],
                        rowsall.ap()[64 + 32 * kk2:96 + 32 * kk2, :],
                        r=("sintab" if kk2 == 0 else "costab",),
                        w=("ropedr%d" % kk2,))
        DMA("scalar", scrF[:, 0:2048], rope_dr.ap()[0],
            r=("ropedr0",), w=("scrF.mb", "scrF.srep"))
        DMA("scalar", scrF[:, 2048:4096], rope_dr.ap()[1],
            r=("ropedr1",), w=("scrF.sb", "scrF.crep"))
        G_(lambda e: e.iota(scrG_i[:, 0:1], pattern=[[0, 1]], base=0,
                            channel_multiplier=1), r=(), w=("scrG.a",))
        p.emit("vector", lambda e: e.tensor_scalar(
            out=scrG_i[:, 1:2], in0=scrG_i[:, 0:1], scalar1=1, scalar2=None,
            op0=OP.bitwise_and), ("scrG.a",), ("scrG.a",), "op")
        vts(small[:, C_SGN:C_SGN + 1], scrG_i[:, 1:2], 2.0, OP.mult,
            -1.0, OP.add, r=("scrG.a",), w=("sgnc",))
        vts(scrF[:, 0:2048], scrF[:, 0:2048], small[:, C_SGN:C_SGN + 1],
            OP.mult, None, None, r=("scrF.srep", "sgnc"), w=("scrF.srep",))

        def load_w(name, kpref):
            for c in range(8):
                DMA("gpsimd", wA[:, c * 1024:(c + 1) * 1024],
                    dI[name].ap()[c * 128:(c + 1) * 128, :],
                    r=(), w=("%s%d" % (kpref, c), "wA.%d" % c))

        load_w("wv_t", "wv")
        for t_ in range(NB):
            pv0 = 4 + (t_ % 2) * 2
            for ck in range(8):
                for hf in range(2):
                    mm(pm[pv0 + hf][:, 0:512],
                       slotA[:, ck * 2048 + t_ * 128: ck * 2048 + (t_ + 1) * 128],
                       wA[:, ck * 1024 + hf * 512: ck * 1024 + (hf + 1) * 512],
                       start=(ck == 0), stop=(ck == 7),
                       r=("xln_%d" % ck, "wv%d" % ck),
                       w=("pm%d" % (pv0 + hf),))
            for hf in range(2):
                vcopy(slotB[:, t_ * 1024 + hf * 512: t_ * 1024 + (hf + 1) * 512],
                      pm[pv0 + hf][:, 0:512],
                      r=("pm%d" % (pv0 + hf),),
                      w=("vtm%d" % t_, "slotB.v", "pm%d" % (pv0 + hf)))
        DMA("gpsimd", kv_agin.ap()[1], slotB[:],
            r=tuple("vtm%d" % t_ for t_ in range(NB)), w=("agin.v",))

        load_w("wk_t", "wk")
        def rope_chunk(ce, tc, wpref, dst_bf, dkeys, pk, pmm):
            t0 = scrG[:, 0:512]
            t1 = scrG[:, 512:1024]
            t2 = scrG[:, 1024:1536]
            vcopy(t0, pmm[:, 0:512], r=(pk,), w=("scrG.a", pk))
            p.emit("vector", lambda e, a=t0, b=t1: e.stream_shuffle(
                out=b, in_=a, mask=SWAPM), ("scrG.a",), ("scrG.b",), "op")
            vtt(t2, t0, scrF[:, 2048 + tc * 512:2048 + (tc + 1) * 512],
                OP.mult, r=("scrG.a", "scrF.crep"), w=("scrG.b2",))
            vtt(t1, t1, scrF[:, tc * 512:(tc + 1) * 512], OP.mult,
                r=("scrG.b", "scrF.srep"), w=("scrG.b",))
            vtt(dst_bf, t2, t1, OP.add, r=("scrG.b2", "scrG.b"), w=dkeys)
        for tc in range(4):
            for ce in range(8):
                pk = "pm%d" % (ce % 2)
                pmm = pm[ce % 2]
                for ck in range(8):
                    mm(pmm[:, 0:512],
                       wA[:, ck * 1024 + ce * 128: ck * 1024 + (ce + 1) * 128],
                       slotA[:, ck * 2048 + tc * 512: ck * 2048 + (tc + 1) * 512],
                       start=(ck == 0), stop=(ck == 7),
                       r=("xln_%d" % ck, "wk%d" % ck), w=(pk,))
                dst = qtm[:, (tc % 2) * 512:(tc % 2 + 1) * 512]
                rope_chunk(ce, tc, "wk", dst, ("qtm.%d" % (tc % 2),), pk, pmm)
                DMA("gpsimd", kv_agin.ap()[0][:, ce * 2048 + tc * 512:
                                              ce * 2048 + (tc + 1) * 512],
                    dst, r=("qtm.%d" % (tc % 2),), w=("agin.k%d%d" % (ce, tc),))
        CC("AllGather", OP.bypass, [kv_agin.ap()], [kv_agout.ap()],
           r=("agin.v",) + tuple("agin.k%d%d" % (ce, tc)
                                 for ce in range(8) for tc in range(4)),
           w=("agout",))
        load_w("wq_t", "wq")
        for ce in range(8):
            for tc in range(4):
                pk = "pm%d" % (2 + tc % 2)
                pmm = pm[2 + tc % 2]
                for ck in range(8):
                    mm(pmm[:, 0:512],
                       wA[:, ck * 1024 + ce * 128: ck * 1024 + (ce + 1) * 128],
                       slotA[:, ck * 2048 + tc * 512: ck * 2048 + (tc + 1) * 512],
                       start=(ck == 0), stop=(ck == 7),
                       r=("xln_%d" % ck, "wq%d" % ck), w=(pk,))
                rope_chunk(ce, tc, "wq",
                           slotB[:, ce * 2048 + tc * 512:
                                 ce * 2048 + (tc + 1) * 512],
                           ("q_%d" % ce, "slotB.v", "agin.v"), pk, pmm)
        if DBG == "q":
            dbg_dump(slotB[:], 128 * 16384,
                     tuple("q_%d" % c for c in range(8)))
        if DBG == "xln":
            dbg_dump(slotA[:], 128 * 16384,
                     tuple("xln_%d" % c for c in range(8)))
        if ATT_SUB < 2:
            p.barrier()
            return
        V_(lambda e: e.memset(scrF_bf[:, 0:4160], 1.0),
           r=(), w=("vg0", "vg1", "scrF.att", "scrF.srep", "scrF.crep"))
        p.emit("vector", lambda e: e.nop(), (),
               tuple("xln_%d" % c for c in range(8)) + ("xlndead",), "op")
        p.barrier()
        qst = scrG_bf[0:64, 2048:4096]
        for h in range(ATT_HEADS):
            kb, vb = h % 2, h % 2
            kk = "wAk%d" % kb
            vk = "vg%d" % vb
            kg = wA.ap()[0:64, kb * 4096:(kb + 1) * 4096]
            hp = (h % 2) * 64
            for g2 in range(2):
                DMA("sync", kg.rearrange("p (j i) -> p j i", i=256)[:, :,
                                                                   g2 * 128:
                                                                   (g2 + 1) * 128],
                    kv_agout.ap()[g2, 0, hp:hp + 64,
                                  (h // 2) * 2048:(h // 2 + 1) * 2048]
                    .rearrange("p (j i) -> p j i", i=128),
                    r=("agout",), w=(kk,))
            vg = scrF_bf[:, vb * 2080:(vb + 1) * 2080]
            for g2 in range(2):
                DMA("sync", vg.rearrange("p (b e) -> p b e", e=130)[:, :,
                                                                   g2 * 65:
                                                                   g2 * 65 + 64],
                    kv_agout.ap()[g2, 1, :, :]
                    .rearrange("p (j e) -> p j e", e=1024)[:, :, h * 64:
                                                           (h + 1) * 64],
                    r=("agout",), w=(vk,))
            if hp == 0:
                qsrc = slotB.ap()[0:64, (h // 2) * 2048:(h // 2 + 1) * 2048]
                qkey = "q_%d" % (h // 2)
            else:
                vcopy(qst, slotB[hp:hp + 64, (h // 2) * 2048:(h // 2 + 1) * 2048],
                      r=("q_%d" % (h // 2),), w=("qst",))
                qsrc, qkey = qst, "qst"
            cnt = [0]

            def emit_pv(B):
                J = B >> 1
                pT = (scrF_bf[:, 4160:6208] if B % 2 == 0
                      else scrG_bf[:, 0:2048])
                for j4 in range(J // 4, 4):
                    jlo = max(J, j4 * 4)
                    c0 = jlo - j4 * 4
                    n5 = 512 - c0 * 128
                    opk = "pm%d" % (4 + j4)
                    first_b = (B == 0)
                    last_b = (B == 8 * j4 + 7)
                    rk = tuple("pT%d.%d" % (B % 2, q)
                               for q in range((jlo - J) // 4,
                                              (j4 * 4 + 3 - J) // 4 + 1))
                    rhs = pT[:, (jlo - J) * 128:(jlo - J) * 128 + n5]
                    if h % 2 == 0:
                        mm(pm[4 + j4][0:65, c0 * 128:512],
                           vg[:, B * 65:(B + 1) * 65], rhs,
                           start=first_b, stop=last_b,
                           r=(vk,) + rk, w=(opk,))
                    else:
                        mm(pm[4 + j4][64:128, c0 * 128:512],
                           vg[:, B * 65:B * 65 + 64], rhs,
                           start=first_b, stop=last_b,
                           r=(vk,) + rk, w=(opk,))
                        mm(pm[4 + j4][0:1, c0 * 128:512],
                           onesc_bf[:], rhs,
                           start=first_b, stop=last_b,
                           r=("ones",) + rk, w=(opk,))

            for B in range(32):
                J = B >> 1
                Wd = T - J * 128
                pT = (scrF_bf[:, 4160:6208] if B % 2 == 0
                      else scrG_bf[:, 0:2048])
                nch = (Wd + 511) // 512
                for q5 in range(nch):
                    w5 = min(512, Wd - q5 * 512)
                    spk = "pm%d" % (cnt[0] % 2)
                    spm = pm[cnt[0] % 2]
                    cnt[0] += 1
                    mm(spm[:, 0:w5], kg[:, B * 128:(B + 1) * 128],
                       qsrc[:, J * 128 + q5 * 512: J * 128 + q5 * 512 + w5],
                       start=True, stop=True,
                       r=(kk, qkey), w=(spk,))
                    act(pT[:, q5 * 512:q5 * 512 + w5], spm[:, 0:w5], AF.Exp,
                        scale=0.125, r=(spk,),
                        w=("pT%d.%d" % (B % 2, q5),))
                gph = B & 1
                vtt(pT[:, 0:128], pT[:, 0:128],
                    masks[:, gph * 128:(gph + 1) * 128], OP.mult,
                    r=("pT%d.0" % (B % 2), "mask%d" % gph),
                    w=("pT%d.0" % (B % 2),))
                if B > 0:
                    emit_pv(B - 1)
            emit_pv(31)
            dp = 64 if h % 2 == 0 else 0
            qtm_f = qtm.ap().bitcast(F32)
            for j4 in range(4):
                opk = "pm%d" % (4 + j4)
                vcopy(slotA[hp:hp + 64, (h // 2) * 2048 + j4 * 512:
                            (h // 2) * 2048 + (j4 + 1) * 512],
                      pm[4 + j4][hp:hp + 64, 0:512],
                      r=(opk, "xlndead"), w=("oall%d" % (h // 2), "slotA.xb"))
                vcopy(qtm_f[dp:dp + 1, 0:512], pm[4 + j4][dp:dp + 1, 0:512],
                      r=(opk,), w=("dstage", opk))
                DMA("scalar", rowsall.ap()[64 + h:65 + h,
                                           j4 * 512:(j4 + 1) * 512],
                    qtm_f[dp:dp + 1, 0:512],
                    r=("dstage",), w=("drow%d" % h, "sintab", "dstage"))
        if DBG == "oall":
            dbg_dump(slotA[:], 128 * 16384,
                     tuple("oall%d" % c for c in range(8)))
        if DBG == "dinv":
            dbg_dump(rowsall.ap().bitcast(BF16)[64:80, :], 16 * 4096,
                tuple("drow%d" % hh for hh in range(ATT_HEADS)))
        if ATT_SUB < 3:
            p.barrier()
            return
        p.barrier()
        p.emit("vector", lambda e: e.reciprocal(out=rowsall[64:80, :],
                                                in_=rowsall[64:80, :]),
               tuple("drow%d" % h for h in range(ATT_HEADS)), ("dinv",), "op")
        DMA("scalar", d_dr.ap(), rowsall.ap()[64:80, :],
            r=("dinv",), w=("ddr",))
        for ce in range(8):
            DMA("scalar", scrG[:, 0:2048],
                d_dr.ap()[2 * ce:2 * ce + 2].unsqueeze(1)
                .broadcast_to([2, 64, 2048]),
                r=("ddr",) + (("osc%d" % (ce - 1),) if ce > 0 else ()),
                w=("dbc",))
            vtt(slotA[:, ce * 2048:(ce + 1) * 2048],
                slotA[:, ce * 2048:(ce + 1) * 2048], scrG[:, 0:2048],
                OP.mult, r=("dbc", "oall%d" % ce), w=("osc%d" % ce,))
        load_w("wo_t", "wo")
        for tc in range(4):
            for ce in range(8):
                pk = "pm%d" % (ce % 2)
                pmm = pm[ce % 2]
                for ck in range(8):
                    mm(pmm[:, 0:512],
                       wA[:, ck * 1024 + ce * 128: ck * 1024 + (ce + 1) * 128],
                       slotA[:, ck * 2048 + tc * 512: ck * 2048 + (tc + 1) * 512],
                       start=(ck == 0), stop=(ck == 7),
                       r=("osc%d" % ck, "wo%d" % ck), w=(pk,))
                vtt(x[:, ce * 2048 + tc * 512: ce * 2048 + (tc + 1) * 512],
                    x[:, ce * 2048 + tc * 512: ce * 2048 + (tc + 1) * 512],
                    pmm[:, 0:512], OP.add, r=(pk, "x%d" % ce),
                    w=("x%d" % ce, pk))
        p.barrier()

    if STOP_AFTER >= 2:
        attention()

    if STOP_AFTER >= 3:
        l0ffn(2)


    def moe(app):
        u = ln_stats(want_rows=True)
        ln_fm_xln(u)
        p.emit("vector", lambda e: e.nop(), ("scrG.a", "scrG.b"),
               ("scrG.a0", "scrG.a1", "scrG.a2", "scrG.a3"), "op")
        a_ = "M%d" % app
        DMA("gpsimd", qtm[:, 0:128].rearrange("p (c e) -> p c e", e=16),
            dI["gate_wt"].ap().rearrange("(c p) e -> p c e", p=128),
            r=(), w=("gwt", "qtm.0"))
        for q in range(4):
            pk = "pm%d" % q
            for ck in range(8):
                mm(pm[q][0:16, 0:512], qtm[:, ck * 16:(ck + 1) * 16],
                   slotA[:, ck * 2048 + q * 512: ck * 2048 + (q + 1) * 512],
                   start=(ck == 0), stop=(ck == 7),
                   r=("gwt", "xln_%d" % ck), w=(pk,))
            vts(scrG[0:16, q * 512:(q + 1) * 512], pm[q][0:16, 0:512],
                small[0:16, 504:505], OP.add, None, None,
                r=(pk, "ebc"), w=("gbia" + a_,))
            act(scrF[0:16, 2048 + q * 512:2048 + (q + 1) * 512],
                pm[q][0:16, 0:512],
                AF.Sigmoid, r=(pk,), w=("gsco" + a_, pk))
        for t_ in range(NB):
            tpose(pm[4][:, t_ * 16:(t_ + 1) * 16],
                  scrG[0:16, t_ * 128:(t_ + 1) * 128], identf[0:16, 0:16],
                  r=("gbia" + a_, "identf"), w=("pm4",))
            tpose(pm[5][:, t_ * 16:(t_ + 1) * 16],
                  scrF[0:16, 2048 + t_ * 128:2048 + (t_ + 1) * 128],
                  identf[0:16, 0:16],
                  r=("gsco" + a_, "identf"), w=("pm5",))
        bia3 = pm[4].ap()[:, 0:256].rearrange("p (t e) -> p t e", e=16)
        sco3 = pm[5].ap()[:, 0:256].rearrange("p (t e) -> p t e", e=16)
        sf3 = lambda c0: scrF[:, c0:c0 + 256].rearrange(
            "p (t e) -> p t e", e=16)
        col3 = lambda c0: scrF[:, c0:c0 + 16].unsqueeze(2).broadcast_to(
            [128, 16, 16])
        red(scrF[:, 1280:1296], bia3, AX.X, OP.max, r=("pm4",),
            w=("gm1" + a_,))
        vtt(sf3(0), bia3, col3(1280), OP.is_ge, r=("pm4", "gm1" + a_),
            w=("scrF.q0",))
        vstt(scrF[:, 256:512], scrF[:, 0:256], -1e30, pm[4].ap()[:, 0:256],
             OP.mult, OP.add, r=("scrF.q0", "pm4"), w=("scrF.q1",))
        red(scrF[:, 1296:1312], sf3(256), AX.X, OP.max, r=("scrF.q1",),
            w=("gm2" + a_,))
        vtt(sf3(512), sf3(256), col3(1296), OP.is_ge,
            r=("scrF.q1", "gm2" + a_), w=("scrF.q2",))
        vtt(scrF[:, 0:256], scrF[:, 0:256], scrF[:, 512:768], OP.add,
            r=("scrF.q0", "scrF.q2"), w=("scrF.q0",))
        vtt(scrF[:, 256:512], scrF[:, 0:256], pm[5].ap()[:, 0:256], OP.mult,
            r=("scrF.q0", "pm5"), w=("scrF.q1", "pm5"))
        red(scrF[:, 1312:1328], sf3(256), AX.X, OP.add, r=("scrF.q1",),
            w=("gws" + a_,))
        vts(scrF[:, 1328:1344], scrF[:, 1312:1328], 1e-10, OP.add, None, None,
            r=("gws" + a_,), w=("gwp" + a_,))
        p.emit("vector", lambda e: e.reciprocal(out=scrF[:, 1344:1360],
                                                in_=scrF[:, 1328:1344]),
               ("gwp" + a_,), ("gwi" + a_,), "op")
        vtt(sf3(512), sf3(256), col3(1344), OP.mult,
            r=("scrF.q1", "gwi" + a_), w=("scrF.q2",))
        for t_ in range(NB):
            k2 = 4 + t_ % 2
            tpose(pm[k2][0:16, 0:128], scrF[:, 512 + t_ * 16:512 + (t_ + 1) * 16],
                  identf[:], r=("scrF.q2", "identf"), w=("pm%d" % k2,))
            vcopy(rowsall_bf[64:80, t_ * 128:(t_ + 1) * 128],
                  pm[k2][0:16, 0:128], r=("pm%d" % k2,),
                  w=("gwfm" + a_, "drow%d" % (t_ % ATT_HEADS), "pm%d" % k2))
        DMA("gpsimd", gw_dr.ap(), rowsall_bf[64:80, 0:2048],
            r=("gwfm" + a_,), w=("gwdr" + a_,))
        DMA("gpsimd", rowsall_bf[64:80, 2048:3072], dI["b2"].ap(),
            r=(), w=("b2sb",))

        def load_expert(e_, buf):
            WT = wA if buf == 0 else wdT
            w2c0 = 0 if buf == 0 else 12288
            DMA("gpsimd",
                WT.ap()[:, 0:4096].rearrange("p (c r) -> p c r", r=512),
                dI["w1t"].ap()[e_].rearrange("(c p) r -> p c r", p=128),
                r=(), w=("w1b%d" % buf,))
            DMA("gpsimd",
                WT.ap()[:, 4096:8192].rearrange("p (c r) -> p c r", r=512),
                dI["w3t"].ap()[e_].rearrange("(c p) r -> p c r", p=128),
                r=(), w=("w3b%d" % buf,))
            DMA("gpsimd",
                slotB_b[:, w2c0:w2c0 + 4096].rearrange(
                    "p (c r) -> p c r", r=1024),
                dI["w2t"].ap()[e_].rearrange("(c p) r -> p c r", p=128),
                r=(), w=("w2b%d" % buf,))

        def load_shared(half, buf):
            WT = wA if buf == 0 else wdT
            w2c0 = 0 if buf == 0 else 12288
            DMA("gpsimd",
                WT.ap()[:, 0:4096].rearrange("p (c r) -> p c r", r=512),
                dI["sw1t"].ap().rearrange("(c p) r -> p c r", p=128)
                [:, :, half * 512:(half + 1) * 512],
                r=(), w=("w1b%d" % buf,))
            DMA("gpsimd",
                WT.ap()[:, 4096:8192].rearrange("p (c r) -> p c r", r=512),
                dI["sw3t"].ap().rearrange("(c p) r -> p c r", p=128)
                [:, :, half * 512:(half + 1) * 512],
                r=(), w=("w3b%d" % buf,))
            DMA("gpsimd",
                slotB_b[:, w2c0:w2c0 + 4096].rearrange(
                    "p (c r) -> p c r", r=1024),
                dI["sw2t"].ap()[half * 512:(half + 1) * 512, :]
                .rearrange("(c p) r -> p c r", p=128),
                r=(), w=("w2b%d" % buf,))

        def gbc_load(e_, buf):
            DMA("scalar", slotB_b[:, 4096 + buf * 2048:4096 + (buf + 1) * 2048],
                gw_dr.ap()[e_].unsqueeze(0).broadcast_to([128, 2048]),
                r=("gwdr" + a_,), w=("gbc%d" % buf,))

        HB = 8192
        for ie in range(18):
            shared = ie < 2
            e_ = ie - 2
            if shared:
                load_shared(ie, ie % 2)
            else:
                load_expert(e_, ie % 2)
                gbc_load(e_, ie % 2)
            hb = ie % 2
            W1 = wA if hb == 0 else wdT
            W2C0 = 0 if hb == 0 else 12288
            for tc in range(4):
                for dffc in range(4):
                    for ck in range(8):
                        mm(pm[dffc][:, 0:512],
                           W1[:, ck * 512 + dffc * 128: ck * 512 + (dffc + 1) * 128],
                           slotA[:, ck * 2048 + tc * 512: ck * 2048 + (tc + 1) * 512],
                           start=(ck == 0), stop=(ck == 7),
                           r=("w1b%d" % hb, "xln_%d" % ck),
                           w=("pm%d" % dffc,))
                        mm(pm[4 + dffc][:, 0:512],
                           W1[:, 4096 + ck * 512 + dffc * 128:
                              4096 + ck * 512 + (dffc + 1) * 128],
                           slotA[:, ck * 2048 + tc * 512: ck * 2048 + (tc + 1) * 512],
                           start=(ck == 0), stop=(ck == 7),
                           r=("w3b%d" % hb, "xln_%d" % ck),
                           w=("pm%d" % (4 + dffc),))
                def bcols(dffc):
                    if shared:
                        return (small[:, 488 + ie * 4 + dffc:
                                      489 + ie * 4 + dffc],
                                small[:, 496 + ie * 4 + dffc:
                                      497 + ie * 4 + dffc],
                                ("sb1c", "sb3c"))
                    return (small[:, e_ * 4 + dffc:e_ * 4 + dffc + 1],
                            small[:, 64 + e_ * 4 + dffc:65 + e_ * 4 + dffc],
                            ("b1c", "b3c"))

                def htgt(dffc):
                    return slotB_b[:, HB + hb * 2048 + dffc * 512:
                                   HB + hb * 2048 + (dffc + 1) * 512]
                for dffc in range(4):
                    b1c_, b3c_, bk = bcols(dffc)
                    act(scrG[:, dffc * 512:(dffc + 1) * 512], pm[dffc][:, 0:512],
                        AF.Silu, bias=b1c_, r=("pm%d" % dffc, bk[0]),
                        w=("scrG.a%d" % dffc, "pm%d" % dffc))
                for dffc in range(4):
                    b1c_, b3c_, bk = bcols(dffc)
                    vstt(htgt(dffc), pm[4 + dffc][:, 0:512], b3c_,
                         scrG[:, dffc * 512:(dffc + 1) * 512], OP.add, OP.mult,
                         r=("pm%d" % (4 + dffc), bk[1], "scrG.a%d" % dffc),
                         w=("h%d_%d" % (hb, dffc), "pm%d" % (4 + dffc)))
                if not shared:
                    for dffc in range(4):
                        vtt(htgt(dffc), htgt(dffc),
                            slotB_b[:, 4096 + hb * 2048 + tc * 512:
                                    4096 + hb * 2048 + (tc + 1) * 512],
                            OP.mult,
                            r=("h%d_%d" % (hb, dffc), "gbc%d" % hb),
                            w=("h%d_%d" % (hb, dffc),))
                for sub in range(2):
                    for q in range(4):
                        pq = sub * 4 + q
                        pk = "pm%d" % pq
                        mms = []
                        if shared and ie == 0:
                            for c2 in range(2):
                                mms.append((pm[pq][:, c2 * 256:(c2 + 1) * 256],
                                            brow[0:1, 1024 + (q * 2 + c2) * 128:
                                                 1024 + (q * 2 + c2 + 1) * 128],
                                            ones_bf[0:1, 0:256], ("sb2r",)))
                        if shared and ie == 1:
                            for c2 in range(2):
                                mms.append((pm[pq][:, c2 * 256:(c2 + 1) * 256],
                                            rowsall_bf[64:80,
                                                       2048 + (q * 2 + c2) * 128:
                                                       2048 + (q * 2 + c2 + 1) * 128],
                                            rowsall_bf[64:80,
                                                       tc * 512 + sub * 256:
                                                       tc * 512 + sub * 256 + 256],
                                            ("b2sb", "gwfm" + a_)))
                        for c2 in range(2):
                            for dffc in range(4):
                                mms.append((pm[pq][:, c2 * 256:(c2 + 1) * 256],
                                            slotB[:, W2C0 + dffc * 1024 +
                                                  (q * 2 + c2) * 128:
                                                  W2C0 + dffc * 1024 +
                                                  (q * 2 + c2 + 1) * 128],
                                            slotB_b[:, HB + hb * 2048 +
                                                    dffc * 512 + sub * 256:
                                                    HB + hb * 2048 +
                                                    dffc * 512 + sub * 256 + 256],
                                            ("w2b%d" % hb,
                                             "h%d_%d" % (hb, dffc))))
                        for i2, (o2, l2, r2, rk2) in enumerate(mms):
                            mm(o2, l2, r2, start=(i2 == 0),
                               stop=(i2 == len(mms) - 1), r=rk2, w=(pk,))
                        xv = x.ap().rearrange("p (c t) -> p c t", t=2048)[
                            :, 2 * q:2 * q + 2,
                            tc * 512 + sub * 256: tc * 512 + sub * 256 + 256]
                        vstt(xv, pm[pq][:, 0:512].rearrange(
                            "p (c t) -> p c t", t=256), 0.5, xv,
                            OP.mult, OP.add,
                            r=(pk, "x%d" % (2 * q), "x%d" % (2 * q + 1)),
                            w=("x%d" % (2 * q), "x%d" % (2 * q + 1), pk))
        p.barrier()


    if STOP_AFTER >= 4:
        moe(1)

    def mamba():
        u = ln_stats(want_rows=True)
        ln_fm_xln(u)
        DMA("gpsimd", scrG_bf[:, 1024:2560].rearrange(
            "p (c r) -> p c r", r=96),
            dI["wxp"].ap().rearrange("(c p) r -> p c r", p=128),
            r=(), w=("wxp", "scrG.b"))
        DMA("gpsimd", slotB_b[0:64, 8192:10240], dI["wdt"].ap(),
            r=(), w=("wdtsb", "slotB.wdt"))
        for qt in range(4):
            for ck in range(8):
                DMA("gpsimd", wA[:, ck * 1024:(ck + 1) * 1024],
                    dI["win"].ap()[ck * 128:(ck + 1) * 128,
                                   qt * 1024:(qt + 1) * 1024],
                    r=(), w=("wi.%d" % ck,))
            for tc in range(4):
                for oc in range(8):
                    pq = oc % 4
                    pk = "pm%d" % pq
                    for ck in range(8):
                        mm(pm[pq][:, 0:512],
                           wA[:, ck * 1024 + oc * 128: ck * 1024 + (oc + 1) * 128],
                           slotA[:, ck * 2048 + tc * 512: ck * 2048 + (tc + 1) * 512],
                           start=(ck == 0), stop=(ck == 7),
                           r=("wi.%d" % ck, "xln_%d" % ck), w=(pk,))
                    sl_ = qtm[:, (oc % 2) * 512:(oc % 2 + 1) * 512]
                    vcopy(sl_, pm[pq][:, 0:512], r=(pk,),
                          w=("xzst%d" % (oc % 2), pk))
                    ch0 = qt * 1024 + oc * 128
                    if qt < 2:
                        DMA("gpsimd", st_xin.ap()[ch0:ch0 + 128,
                                                  tc * 512:(tc + 1) * 512],
                            sl_, r=("xzst%d" % (oc % 2),),
                            w=("stxin%d_%d" % (tc, (ch0 // 128)),
                               "xzst%d" % (oc % 2)))
                    else:
                        zc = ch0 - 2048
                        DMA("gpsimd", mz_in[tc].ap()[zc // 1024, 2,
                                                     (zc % 1024):
                                                     (zc % 1024) + 128, :],
                            sl_, r=("xzst%d" % (oc % 2),),
                            w=("mzz%d" % tc, "xzst%d" % (oc % 2)))
        DMA("sync", hal_in.ap().rearrange("c (j k) -> c j k", k=3),
            st_xin.ap().rearrange("c (j i) -> c j i", i=128)[:, :, 125:128],
            r=tuple("stxin%d_%d" % (tc, cc) for tc in range(4)
                    for cc in range(16)),
            w=("halin",))
        CC("AllGather", OP.bypass, [hal_in.ap()], [hal_out.ap()],
           r=("halin",), w=("halout",))
        for tc in range(4):
            for cc in range(16):
                pad = scrF_bf[:, 0:524].rearrange("p (j i) -> p j i", i=131)
                DMA("sync", pad[:, :, 3:131],
                    st_xin.ap()[cc * 128:(cc + 1) * 128,
                                tc * 512:(tc + 1) * 512]
                    .rearrange("p (j i) -> p j i", i=128),
                    r=("stxin%d_%d" % (tc, cc), "convdone"), w=("pad",))
                DMA("sync", scrF_bf[:, 524:536],
                    hal_out.ap()[0, cc * 128:(cc + 1) * 128,
                                 tc * 12:(tc + 1) * 12],
                    r=("halout",), w=("candA",))
                if tc == 0:
                    V_(lambda e: e.memset(scrF_bf[:, 536:539], 0.0),
                       r=(), w=("candB",))
                    DMA("sync", scrF_bf[:, 539:548],
                        hal_out.ap()[1, cc * 128:(cc + 1) * 128, 0:9],
                        r=("halout",), w=("candB",))
                else:
                    DMA("sync", scrF_bf[:, 536:548],
                        hal_out.ap()[1, cc * 128:(cc + 1) * 128,
                                     tc * 12 - 3:(tc + 1) * 12 - 3],
                        r=("halout",), w=("candB",))
                vts(scrF_bf[:, 548:560], scrF_bf[:, 524:536],
                    gcols[:, 0:1], OP.mult, None, None,
                    r=("candA", "gc"), w=("candt",))
                vstt(pad[:, :, 0:3],
                     scrF_bf[:, 536:548].rearrange("p (j i) -> p j i", i=3),
                     gcols[:, 1:2],
                     scrF_bf[:, 548:560].rearrange("p (j i) -> p j i", i=3),
                     OP.mult, OP.add, r=("candB", "gc", "candt", "pad"),
                     w=("pad",))
                cwc = lambda k: small[:, 128 + cc * 4 + k:129 + cc * 4 + k]
                vts(scrG[:, 0:512].rearrange("p (j i) -> p j i", i=128),
                    pad[:, :, 0:128], cwc(0), OP.mult, None, None,
                    r=("pad", "cwc"), w=("scrG.cv",))
                for k in range(1, 4):
                    vstt(scrG[:, 0:512].rearrange("p (j i) -> p j i", i=128),
                         pad[:, :, k:k + 128], cwc(k),
                         scrG[:, 0:512].rearrange("p (j i) -> p j i", i=128),
                         OP.mult, OP.add, r=("pad", "cwc", "scrG.cv"),
                         w=("scrG.cv",))
                act(slotB_b[:, cc * 512:(cc + 1) * 512], scrG[:, 0:512],
                    AF.Silu, bias=small[:, 192 + cc:193 + cc],
                    r=("scrG.cv", "cbc"), w=("xc%d" % cc, "slotB.xc"))
                DMA("gpsimd", mz_in[tc].ap()[cc // 8, 0,
                                             (cc % 8) * 128:(cc % 8) * 128 + 128,
                                             :],
                    slotB_b[:, cc * 512:(cc + 1) * 512],
                    r=("xc%d" % cc,), w=("mzx%d" % tc,))
            pk = "pm6"
            for ck in range(16):
                mm(pm[6][0:96, 0:512],
                   scrG_bf[:, 1024 + ck * 96:1024 + (ck + 1) * 96],
                   slotB_b[:, ck * 512:(ck + 1) * 512],
                   start=(ck == 0), stop=(ck == 15),
                   r=("wxp", "xc%d" % ck), w=(pk,))
            vcopy(qtm[0:64, 0:512], pm[6][0:64, 0:512], r=(pk,),
                  w=("dtr", "qtm.0", "xzst0", "xzst1"))
            vcopy(scrG_bf[64:96, 2560:3072], pm[6][64:96, 0:512],
                  r=(pk,), w=("bcst", pk, "scrG.bc"))
            DMA("gpsimd", bc_in.ap()[:, tc * 512:(tc + 1) * 512],
                scrG_bf[64:96, 2560:3072], r=("bcst",), w=("bcin%d" % tc,))
            for oc in range(16):
                pq = 7
                pk = "pm7"
                mm(pm[7][:, 0:512], slotB_b[0:64, 8192 + oc * 128:
                                            8192 + (oc + 1) * 128],
                   qtm[0:64, 0:512], start=True, stop=True,
                   r=("wdtsb", "dtr"), w=(pk,))
                sl_ = slotB_b[:, 10240 + (oc % 2) * 512:
                              10240 + (oc % 2 + 1) * 512]
                act(scrF[:, 1024:1536], pm[7][:, 0:512], AF.Exp,
                    bias=small[:, 208 + oc:209 + oc],
                    r=(pk, "bdtc"), w=("scrF.sp", pk))
                vts(scrF[:, 1024:1536], scrF[:, 1024:1536], 1.0, OP.add,
                    None, None, r=("scrF.sp",), w=("scrF.sp",))
                act(sl_, scrF[:, 1024:1536], AF.Ln,
                    r=("scrF.sp",), w=("dtst%d" % (oc % 2),))
                DMA("gpsimd", mz_in[tc].ap()[oc // 8, 1,
                                             (oc % 8) * 128:(oc % 8) * 128 + 128,
                                             :],
                    sl_, r=("dtst%d" % (oc % 2),),
                    w=("mzd%d" % tc, "dtst%d" % (oc % 2)))
            p.emit("vector", lambda e: e.nop(),
                   tuple("xc%d" % cc for cc in range(16)), ("convdone",), "op")
            CC("AllGather", OP.bypass, [mz_in[tc].ap()], [mz_out[tc].ap()],
               r=("mzx%d" % tc, "mzd%d" % tc, "mzz%d" % tc),
               w=("mzo%d" % tc,))
        CC("AllGather", OP.bypass, [bc_in.ap()], [bc_out.ap()],
           r=tuple("bcin%d" % tc for tc in range(4)), w=("bcout",))
        for g2 in range(2):
            DMA("sync", bc_rep.ap().rearrange("r (b s i) -> r b s i",
                                              s=2, i=128)[:, :, g2, :],
                bc_out.ap()[g2].rearrange("r (b i) -> r b i", i=128),
                r=("bcout",), w=("bcrep",))
        p.barrier()

        for ck in range(8):
            DMA("gpsimd", wA[:, ck * 1024:(ck + 1) * 1024],
                dI["wout_s"].ap()[ck * 128:(ck + 1) * 128, :],
                r=(), w=("wo.%d" % ck,))
        XC, DT, DX, AA, BB, HH, ZZ = 0, 4096, 8192, 12288, 13312, 14336, 15360
        V_(lambda e: e.memset(hcarry[:], 0.0), r=(),
           w=tuple("hc%d_%d" % (n, dc) for n in range(NSTATE)
                   for dc in range(8)))
        for gc in range(8):
            tci, sb2_ = gc // 2, (gc % 2) * 256
            CA = 12288

            def merge_ld(which, row0, nd, dstbase, kks, cbase=None,
                         extra_r=()):
                CAx = CA if cbase is None else cbase
                w_ = nd * 256
                for g2 in range(2):
                    for s2 in range(2):
                        DMA("sync",
                            slotB_b[:, CAx + (s2 * 2 + g2) * w_:
                                    CAx + (s2 * 2 + g2 + 1) * w_]
                            .rearrange("p (d i) -> p d i", i=256),
                            mz_out[tci].ap()[g2, s2, which,
                                             row0 * 128:(row0 + nd) * 128,
                                             sb2_:sb2_ + 256]
                            .rearrange("(d p) i -> p d i", p=128),
                            r=("mzo%d" % tci, "scandone", "cands", "sa0",
                               "sa1", "sb0", "sb1", "sh0", "sh1") + extra_r,
                            w=("cands",))
                for g2 in range(2):
                    ca = slotB_b[:, CAx + g2 * w_:CAx + (g2 + 1) * w_]\
                        .rearrange("p (d j i) -> p d j i", j=2, i=128)
                    cb = slotB_b[:, CAx + (2 + g2) * w_:CAx + (3 + g2) * w_]\
                        .rearrange("p (d j i) -> p d j i", j=2, i=128)
                    tmp = scrG[:, 0:w_].rearrange(
                        "p (d j i) -> p d j i", j=2, i=128)
                    vts(tmp, ca, gcols[:, 1:2], OP.mult, None, None,
                        r=("cands", "gc"), w=("scrG.sel",))
                    dst = slotB_b[:, dstbase:dstbase + nd * 512]\
                        .rearrange("p (d j s i) -> p d j s i", j=2, s=2,
                                   i=128)[:, :, :, g2, :]
                    vstt(dst, cb, gcols[:, 0:1], tmp, OP.mult, OP.add,
                         r=("cands", "gc", "scrG.sel"), w=kks + ("cands",))
            for dhalf in range(2):
                merge_ld(0, dhalf * 4, 4, XC + dhalf * 2048,
                         tuple("sxc%d" % dc for dc in
                               range(dhalf * 4, dhalf * 4 + 4)))
                merge_ld(1, dhalf * 4, 4, DT + dhalf * 2048,
                         tuple("sdt%d" % dc for dc in
                               range(dhalf * 4, dhalf * 4 + 4)))
            for dc in range(8):
                vtt(slotB_b[:, DX + dc * 512:DX + (dc + 1) * 512],
                    slotB_b[:, XC + dc * 512:XC + (dc + 1) * 512],
                    slotB_b[:, DT + dc * 512:DT + (dc + 1) * 512],
                    OP.mult, r=("sxc%d" % dc, "sdt%d" % dc),
                    w=("sdx%d" % dc,))
            for n in range(NSTATE):
                for wbc, wb0, kk in ((0, 0, "brep"), (1, 4096, "crep")):
                    DMA("scalar", scrF_bf[:, wb0 + (n % 4) * 1024:
                                          wb0 + (n % 4) * 1024 + 512],
                        bc_rep.ap()[wbc * 16 + n, gc * 512:(gc + 1) * 512]
                        .unsqueeze(0).broadcast_to([128, 512]),
                        r=("bcrep", "%s%d" % (kk, n % 4)),
                        w=("%s%d" % (kk, n % 4),))
                for dcp in range(4):
                    pair = (2 * dcp, 2 * dcp + 1)
                    for dc in pair:
                        ab = dc % 2
                        acol_ = small[:, 224 + dc * 16 + n:225 + dc * 16 + n]
                        act(slotB_b[:, AA + ab * 512:AA + (ab + 1) * 512],
                            slotB_b[:, DT + dc * 512:DT + (dc + 1) * 512],
                            AF.Exp, scale=acol_,
                            r=("sdt%d" % dc, "acol", "cands"),
                            w=("sa%d" % ab,))
                    for dc in pair:
                        ab = dc % 2
                        vtt(slotB_b[:, BB + ab * 512:BB + (ab + 1) * 512],
                            slotB_b[:, DX + dc * 512:DX + (dc + 1) * 512],
                            scrF_bf[:, (n % 4) * 1024:(n % 4) * 1024 + 512],
                            OP.mult,
                            r=("sdx%d" % dc, "brep%d" % (n % 4), "cands"),
                            w=("sb%d" % ab,))
                    for dc in pair:
                        ab = dc % 2
                        hslot = slotB_b[:, HH + ab * 512:HH + (ab + 1) * 512]
                        p.emit("vector",
                               lambda e, o=hslot,
                               a0=slotB_b[:, AA + ab * 512:AA + (ab + 1) * 512],
                               b0=slotB_b[:, BB + ab * 512:BB + (ab + 1) * 512],
                               i0=hcarry[:, n * 8 + dc:n * 8 + dc + 1]:
                               e.tensor_tensor_scan(out=o, data0=a0, data1=b0,
                                                    initial=i0, op0=OP.mult,
                                                    op1=OP.add),
                               ("sa%d" % ab, "sb%d" % ab,
                                "hc%d_%d" % (n, dc), "cands"),
                               ("sh%d" % ab,), "op")
                    for dc in pair:
                        ab = dc % 2
                        hslot = slotB_b[:, HH + ab * 512:HH + (ab + 1) * 512]
                        yv = slotA_f[:, dc * 512:(dc + 1) * 512]
                        crep = scrF_bf[:, 4096 + (n % 4) * 1024:
                                       4096 + (n % 4) * 1024 + 512]
                        if n == 0:
                            vtt(yv, hslot, crep, OP.mult,
                                r=("sh%d" % ab, "crep%d" % (n % 4)),
                                w=("ya%d" % dc, "slotA.xb"))
                        else:
                            vtt(slotB_b[:, BB + ab * 512:BB + (ab + 1) * 512],
                                hslot, crep, OP.mult,
                                r=("sh%d" % ab, "crep%d" % (n % 4)),
                                w=("sb%d" % ab,))
                    if n > 0:
                        for dc in pair:
                            ab = dc % 2
                            yv = slotA_f[:, dc * 512:(dc + 1) * 512]
                            vtt(yv, yv,
                                slotB_b[:, BB + ab * 512:BB + (ab + 1) * 512],
                                OP.add, r=("ya%d" % dc, "sb%d" % ab),
                                w=("ya%d" % dc,))
                    for dc in pair:
                        ab = dc % 2
                        hslot = slotB_b[:, HH + ab * 512:HH + (ab + 1) * 512]
                        vcopy(hcarry[:, n * 8 + dc:n * 8 + dc + 1],
                              hslot[:, 511:512], r=("sh%d" % ab,),
                              w=("hc%d_%d" % (n, dc),))
            for dc in range(8):
                merge_ld(2, dc, 1, ZZ + (dc % 2) * 512,
                         ("sz%d" % (dc % 2),), cbase=HH,
                         extra_r=tuple("ya%d" % d2 for d2 in range(8)) +
                         tuple("hc15_%d" % d2 for d2 in range(8)) +
                         ("sh0", "sh1"))
                zs = slotB_b[:, AA + 0 * 512:AA + 512]
                act(zs, slotB_b[:, ZZ + (dc % 2) * 512:ZZ + (dc % 2 + 1) * 512],
                    AF.Silu, r=("sz%d" % (dc % 2), "sa0"), w=("sa0",))
                yv = slotA_f[:, dc * 512:(dc + 1) * 512]
                vstt(yv, slotB_b[:, XC + dc * 512:XC + (dc + 1) * 512],
                     small[:, 480 + dc:481 + dc], yv, OP.mult, OP.add,
                     r=("sxc%d" % dc, "dpc", "ya%d" % dc),
                     w=("ya%d" % dc,))
                vtt(slotB_b[:, DX + dc * 512:DX + (dc + 1) * 512], yv, zs,
                    OP.mult, r=("ya%d" % dc, "sa0", "sdx%d" % dc),
                    w=("sy3%d" % dc,))
            for dmc in range(8):
                pq = dmc % 4
                pk = "pm%d" % pq
                for ck in range(8):
                    mm(pm[pq][:, 0:512],
                       wA[:, ck * 1024 + dmc * 128: ck * 1024 + (dmc + 1) * 128],
                       slotB_b[:, DX + ck * 512:DX + (ck + 1) * 512],
                       start=(ck == 0), stop=(ck == 7),
                       r=("wo.%d" % ck, "sy3%d" % ck), w=(pk,))
                sl_ = qtm[:, (dmc % 2) * 512:(dmc % 2 + 1) * 512]
                vcopy(sl_, pm[pq][:, 0:512], r=(pk,),
                      w=("wost%d" % (dmc % 2), pk, "qtm.0"))
                for g2 in range(2):
                    DMA("gpsimd",
                        wo_in[tci].ap()[g2, dmc * 128:(dmc + 1) * 128,
                                        sb2_:sb2_ + 256]
                        .rearrange("p (j i) -> p j i", i=128),
                        sl_.rearrange("p (j s i) -> p j s i", s=2, i=128)
                        [:, :, g2, :],
                        r=("wost%d" % (dmc % 2),),
                        w=("woin%d" % tci, "wost%d" % (dmc % 2)))
            p.emit("vector", lambda e: e.nop(),
                   tuple("sy3%d" % dc for dc in range(8)) +
                   tuple("sxc%d" % dc for dc in range(8)) +
                   ("sa0", "sa1", "sb0", "sb1", "sh0", "sh1", "sz0", "sz1",
                    "cands"),
                   ("scandone",), "op")
            if gc % 2 == 1:
                CC("AllGather", OP.bypass, [wo_in[tci].ap()],
                   [wo_out[tci].ap()], r=("woin%d" % tci,),
                   w=("woout%d" % tci,))
        for tci in range(4):
            for g2 in range(2):
                for s2 in range(2):
                    DMA("sync", slotB_b[:, (s2 * 2 + g2) * 4096:
                                        (s2 * 2 + g2 + 1) * 4096]
                        .rearrange("p (c t) -> p c t", t=512),
                        wo_out[tci].ap()[g2, s2].rearrange(
                            "(c p) t -> p c t", p=128),
                        r=("woout%d" % tci, "wfin", "scandone",
                           "wost0", "wost1"), w=("wfb",))
            for s2 in range(2):
                vtt(slotB_b[:, s2 * 4096:s2 * 4096 + 4096],
                    slotB_b[:, s2 * 4096:s2 * 4096 + 4096],
                    slotB_b[:, (s2 + 2) * 4096:(s2 + 3) * 4096],
                    OP.add, r=("wfb",), w=("wfb",))
            gsl = gcols[:, 1:2]
            vts(slotB_b[:, 0:4096], slotB_b[:, 0:4096], gsl, OP.mult,
                None, None, r=("wfb", "gc"), w=("wfb",))
            vstt(slotB_b[:, 0:4096], slotB_b[:, 4096:8192], gcols[:, 0:1],
                 slotB_b[:, 0:4096], OP.mult, OP.add,
                 r=("wfb", "gc"), w=("wfb",))
            xv = x.ap().rearrange("p (c t) -> p c t", t=2048)[
                :, :, tci * 512:(tci + 1) * 512]
            vtt(xv, xv, slotB_b[:, 0:4096].rearrange(
                "p (c t) -> p c t", t=512), OP.add,
                r=("wfb",) + ("x0", "x1", "x2", "x3", "x4", "x5", "x6", "x7"), w=("x0", "x1", "x2", "x3", "x4", "x5", "x6", "x7") + ("wfin",))
        p.barrier()


    if STOP_AFTER >= 5:
        mamba()

    if STOP_AFTER >= 6:
        moe(2)


    if DBG is None:
        DMA("sync", out_fm.ap().rearrange("(c p) t -> p c t", p=128),
            x[:].rearrange("p (c t) -> p c t", t=T),
            r=("x0", "x1", "x2", "x3", "x4", "x5", "x6", "x7"), w=("outw",))
    p.emit("sync", lambda e: e.nop(), ("outw",), (), "op", "final")

    p.prepare()
    with nc.Block() as block:
        @block.tensor
        def _(eng):
            p.emit_stream("tensor", eng, sems)

        @block.vector
        def _(eng):
            p.emit_stream("vector", eng, sems)

        @block.scalar
        def _(eng):
            p.emit_stream("scalar", eng, sems)

        @block.gpsimd
        def _(eng):
            p.emit_stream("gpsimd", eng, sems)

        @block.sync
        def _(eng):
            p.emit_stream("sync", eng, sems)
    es.close()
    return nc


_NC_CACHE = {}


def kernel(**inputs):
    per_core = _shard_host(inputs)
    if "nc" not in _NC_CACHE:
        _NC_CACHE["nc"] = build_nc()
    nc = _NC_CACHE["nc"]
    res = run_bass_kernel_spmd(nc, per_core, core_ids=list(range(8)))
    return _gather_host(res.results)
